# version 18
# baseline (speedup 1.0000x reference)
import contextlib
import numpy as np
import ml_dtypes

import concourse.bass as bass
import concourse.mybir as mybir
from concourse.alu_op_type import AluOpType
from concourse.bass_utils import run_bass_kernel_spmd

F32 = mybir.dt.float32
BF16 = mybir.dt.bfloat16
AF = mybir.ActivationFunctionType
AX = mybir.AxisListType

D = 1024
H = 16
HD = 64
DH = 4096
KCH = 8
MAGIC = 3 * 2.0 ** 22
EPS = 1e-5
NCORES = 8


class Prog:

    ENGINES = ("sync", "tensor", "vector", "scalar", "gpsimd")

    def __init__(self):
        self.items = {e: [] for e in self.ENGINES}
        self.cnt = {}
        self.hw = {e: {} for e in self.ENGINES}

    def now(self, sem):
        return self.cnt.get(sem, 0)

    def on(self, eng, fn, waits=(), inc=None, fence=False):
        w = []
        for sem, val in waits:
            if val is None or val <= 0:
                continue
            if val > self.hw[eng].get(sem, 0):
                self.hw[eng][sem] = val
                w.append((sem, val))
        self.items[eng].append([w, fn, None, fence])
        if inc is not None:
            return self.bump(eng, *inc)
        return None

    def bump(self, eng, sem, k, fence=False):
        assert self.items[eng][-1][2] is None
        self.items[eng][-1][2] = (sem, k)
        if fence:
            self.items[eng][-1][3] = True
        self.cnt[sem] = self.cnt.get(sem, 0) + k
        return self.cnt[sem]

    def emit(self, block, sems):
        for name in self.ENGINES:
            eng_items = self.items[name]

            def make(eng_items=eng_items):
                def body(e):
                    for waits, fn, inc, fence in eng_items:
                        for sem, val in waits:
                            e.wait_ge(sems[sem], val)
                        ins = fn(e)
                        if fence:
                            ins = e.drain()
                        if inc is not None:
                            assert ins is not None, "inc on op returning None"
                            ins.then_inc(sems[inc[0]], inc[1])
                return body

            getattr(block, name)(make())


DBG_TILE = 8


def build(n_tiles, debug=False, dual_store=False):
    assert n_tiles % 2 == 0
    NT = n_tiles
    nc = bass.Bass("TRN2", target_bir_lowering=False, detect_race_conditions=False)

    x_in = nc.dram_tensor("x", [NT * 128, D], F32, kind="ExternalInput").ap()
    wqkvT = nc.dram_tensor("wqkvT", [D, 3 * D], F32, kind="ExternalInput").ap()
    woutT = nc.dram_tensor("woutT", [D, D], F32, kind="ExternalInput").ap()
    w1T = nc.dram_tensor("w1T", [D, DH], F32, kind="ExternalInput").ap()
    w2T = nc.dram_tensor("w2T", [DH, D], F32, kind="ExternalInput").ap()
    ct_in = nc.dram_tensor("ctab", [128, NT * 32], F32, kind="ExternalInput").ap()
    st_in = nc.dram_tensor("stab", [128, NT * 32], F32, kind="ExternalInput").ap()
    eye_in = nc.dram_tensor("eye", [128, 128], BF16, kind="ExternalInput").ap()
    ones_in = nc.dram_tensor("onesc", [128, 4], F32, kind="ExternalInput").ap()
    y_out = nc.dram_tensor("y", [NT * 128, D], F32, kind="ExternalOutput").ap()
    y2_out = (nc.dram_tensor("y2", [NT * 128, D], F32, kind="ExternalOutput").ap()
              if dual_store else None)
    dbg = {}
    if debug:
        for nm, shp, dt in [("d_scalc", [128, 6], F32), ("d_w1q", [128, DH], BF16),
                            ("d_w2q", [128, D], BF16), ("d_kv", [128, D], F32),
                            ("d_ks", [128, D], F32), ("d_qf", [128, D], F32),
                            ("d_x1", [128, D], F32), ("d_xq1", [128, D], BF16),
                            ("d_hact", [128, DH], F32), ("d_att", [128, D], BF16),
                            ("d_u", [128, D], F32), ("d_hq", [128, DH], BF16),
                            ("d_u2", [128, D], F32), ("d_cols", [128, 40], F32),
                            ("d_ccin", [129, D], F32), ("d_ccout", [129, D], F32)]:
            dbg[nm] = nc.dram_tensor(nm, shp, dt, kind="ExternalOutput").ap()

    cc_in = nc.dram_tensor("cc_in", [129, D], F32)
    cc_out = nc.dram_tensor("cc_out", [129, D], F32)
    w2q_hbm = nc.dram_tensor("w2q_hbm", [DH, D], BF16)
    scal_hbm = nc.dram_tensor("scal_hbm", [1, 8], F32)

    es = contextlib.ExitStack()
    _s = es.enter_context

    w1q_s = _s(nc.sbuf_tensor("w1q_s", [128, KCH * DH], BF16))
    wq_s = _s(nc.sbuf_tensor("wq_s", [128, KCH * D], BF16))
    wout_s = _s(nc.sbuf_tensor("wout_s", [128, KCH * D], BF16))
    ct_s = _s(nc.sbuf_tensor("ct_s", [128, NT * 32], F32))
    st_s = _s(nc.sbuf_tensor("st_s", [128, NT * 32], F32))
    ksum_b = _s(nc.sbuf_tensor("ksum_b", [128, D], F32))
    kvaug = _s(nc.sbuf_tensor("kvaug", [128, D], BF16))
    eye_s = _s(nc.sbuf_tensor("eye_s", [128, 128], BF16))
    ones_s = _s(nc.sbuf_tensor("ones_s", [128, 4], F32))
    onesb_s = _s(nc.sbuf_tensor("onesb_s", [128, 1], BF16))
    cols = _s(nc.sbuf_tensor("cols", [128, 40], F32))
    stats = _s(nc.sbuf_tensor("stats", [128, 24], F32))
    scalc = _s(nc.sbuf_tensor("scalc", [128, 8], F32))

    scrA = _s(nc.sbuf_tensor("scrA", [128, D], F32))
    scrB = _s(nc.sbuf_tensor("scrB", [128, D], F32))
    scrC = _s(nc.sbuf_tensor("scrC", [128, D], F32))
    xb16 = [_s(nc.sbuf_tensor(f"xb16_{q}", [128, D], BF16)) for q in range(2)]
    xfm = [_s(nc.sbuf_tensor(f"xfm_{q}", [128, D], BF16)) for q in range(2)]

    p = Prog()
    cp = {}

    def dma(eng, out, in_, sem, waits=()):
        p.on(eng, lambda e, out=out, in_=in_: e.dma_start(out=out, in_=in_), waits=waits)
        return p.bump(eng, sem, 16)

    def rope_ops(e, src, dst, tile_i, tmp=None):
        tmp = tmp if tmp is not None else scrA
        cv = ct_s[:, 32 * tile_i:32 * (tile_i + 1)].unsqueeze(1).broadcast_to([128, H, 32])
        sv = st_s[:, 32 * tile_i:32 * (tile_i + 1)].unsqueeze(1).broadcast_to([128, H, 32])
        r3 = lambda ap: ap.rearrange("t (h two f) -> t h two f", h=H, two=2)
        xe, xo = r3(src[:])[:, :, 0, :], r3(src[:])[:, :, 1, :]
        re, ro = r3(dst[:])[:, :, 0, :], r3(dst[:])[:, :, 1, :]
        t1 = tmp[:, 0:512].rearrange("t (h f) -> t h f", h=H)
        t2 = tmp[:, 512:1024].rearrange("t (h f) -> t h f", h=H)
        e.tensor_tensor(t1, xe, cv, op=AluOpType.mult)
        e.tensor_tensor(t2, xo, sv, op=AluOpType.mult)
        e.tensor_tensor(re, t1, t2, op=AluOpType.subtract)
        e.tensor_tensor(t1, xe, sv, op=AluOpType.mult)
        e.tensor_tensor(t2, xo, cv, op=AluOpType.mult)
        return e.tensor_tensor(ro, t1, t2, op=AluOpType.add)

    with contextlib.ExitStack() as p0:
        wf = [p0.enter_context(nc.sbuf_tensor(f"wf_{i}", [128, DH], F32)) for i in range(2)]
        wqf = [p0.enter_context(nc.sbuf_tensor(f"wqf_{i}", [128, DH], F32)) for i in range(2)]
        wqb = [p0.enter_context(nc.sbuf_tensor(f"wqb_{i}", [128, D], BF16)) for i in range(2)]
        ps_tot = p0.enter_context(nc.psum_tensor("ps_tot", [2, 512], F32))

        dma("sync", ct_s[:], ct_in, "s_ld")
        dma("sync", st_s[:], st_in, "s_ld", waits=[("s_ld", p.now("s_ld"))])
        dma("sync", eye_s[:], eye_in, "s_ld", waits=[("s_ld", p.now("s_ld"))])
        c_consts = dma("sync", ones_s[:], ones_in, "s_ld",
                       waits=[("s_ld", p.now("s_ld"))])
        c_onesb = p.on("vector", lambda e: e.tensor_copy(onesb_s[:], ones_s[:, 0:1]),
                       waits=[("s_ld", c_consts)], inc=("s_v", 1), fence=True)
        for kk in range(KCH):
            dma("gpsimd", wq_s[:, kk * D:(kk + 1) * D],
                wqkvT[128 * kk:128 * (kk + 1), 0:D], "s_wc",
                waits=[("s_wc", p.now("s_wc"))])
            dma("gpsimd", wout_s[:, kk * D:(kk + 1) * D],
                woutT[128 * kk:128 * (kk + 1), :], "s_wc",
                waits=[("s_wc", p.now("s_wc"))])
        c_wcast = p.now("s_wc")

        p.on("vector", lambda e: e.memset(cols[:, 0:2], 0.0), waits=[("s_ld", c_consts)])
        c_wdma = {}
        for idx, (nch, src, width) in enumerate(((KCH, w1T, DH), (32, w2T, D))):
            for r in range(nch):
                key = (idx, r)
                w = [("s_v", c_wdma.get((idx, r - 2)))] if r >= 2 or idx > 0 else ()
                if idx == 1 and r < 2:
                    w = [("s_v", c_wdma.get((0, KCH - 2 + r)))]
                c = dma("sync", wf[r % 2][:, 0:width], src[128 * r:128 * (r + 1), :],
                        "s_l0" if r % 2 == 0 else "s_l1", waits=w)

                lsem = "s_l0" if r % 2 == 0 else "s_l1"
                if r % 2 == 1:
                    c_ab = p.on("scalar", lambda e, r=r, width=width: e.activation(
                        wqf[1][:, 0:width], wf[r % 2][:, 0:width], AF.Abs,
                        accum_out=cols[:, 3:4]),
                        waits=[(lsem, c), ("s_v", c_wdma.get((idx, r - 2)) or
                               (c_wdma.get((0, KCH - 1)) if idx == 1 else None) or 0)],
                        inc=("s_s", 1), fence=True)
                    c_wdma[key] = p.on("vector", lambda e, idx=idx: e.tensor_tensor(
                        cols[:, idx:idx + 1], cols[:, idx:idx + 1], cols[:, 3:4],
                        op=AluOpType.add),
                        waits=[("s_s", c_ab)], inc=("s_v", 1), fence=True)
                else:
                    def absacc(e, r=r, idx=idx, width=width):
                        e.tensor_reduce(cols[:, 2:3], wf[r % 2][:, 0:width], axis=AX.X,
                                        op=AluOpType.add, apply_absolute_value=True)
                        e.drain()
                        return e.tensor_tensor(cols[:, idx:idx + 1], cols[:, idx:idx + 1],
                                               cols[:, 2:3], op=AluOpType.add)
                    c_wdma[key] = p.on("vector", absacc,
                                       waits=[(lsem, c)],
                                       inc=("s_v", 1), fence=True)
        c_sums = p.now("s_v")
        c_tot = p.on("tensor", lambda e: e.matmul(
            ps_tot[0:1, 0:2], ones_s[:, 0:1], cols[:, 0:2], start=True, stop=True),
            waits=[("s_v", c_sums), ("s_ld", c_consts)], inc=("s_t", 1))

        def mk_scal(e):
            e.tensor_scalar(scalc[0:1, 4:6], ps_tot[0:1, 0:2], 1.0 / (D * DH), EPS,
                            op0=AluOpType.mult, op1=AluOpType.max)
            e.drain()
            e.reciprocal(scalc[0:1, 0:1], scalc[0:1, 4:5])
            e.reciprocal(scalc[0:1, 2:3], scalc[0:1, 5:6])
            e.tensor_scalar(scalc[0:1, 1:2], scalc[0:1, 4:5], 1.0 / 127.0, None,
                            op0=AluOpType.mult)
            return e.tensor_scalar(scalc[0:1, 3:4], scalc[0:1, 5:6], 1.0 / 127.0, None,
                                   op0=AluOpType.mult)
        c_scal = p.on("vector", mk_scal, waits=[("s_t", c_tot)], inc=("s_v", 1),
                      fence=True)
        c = dma("gpsimd", scal_hbm.ap()[0:1, 0:6], scalc[0:1, 0:6], "s_sc",
                waits=[("s_v", c_scal)])
        c_scalb = dma("gpsimd", scalc[:, 0:6], scal_hbm.ap()[0:1, 0:6].broadcast_to([128, 6]),
                      "s_sc", waits=[("s_sc", c)])

        c_sq = {"s": {}, "v": {}, "g": {}}
        for idx, (nch, src, width) in enumerate(((KCH, w1T, DH), (32, w2T, D))):
            for r in range(nch):
                gr = r + (KCH if idx == 1 else 0)
                b = gr % 2
                w = ([("s_s", c_sq["s"].get(gr - 2, 0))] if gr >= 2
                     else [("s_v", c_sums)])
                lsem = "s_l0" if b == 0 else "s_l1"
                c = dma("sync", wf[b][:, 0:width], src[128 * r:128 * (r + 1), :],
                        lsem, waits=w)
                sc = scalc[:, 0:1] if idx == 0 else scalc[:, 2:3]
                c_sq["s"][gr] = p.on(
                    "scalar", lambda e, b=b, width=width, sc=sc: e.activation(
                        wqf[b][:, 0:width], wf[b][:, 0:width], AF.Copy, bias=MAGIC,
                        scale=sc),
                    waits=[(lsem, c), ("s_sc", c_scalb),
                           ("s_v", c_sq["v"].get(gr - 2, 0))],
                    inc=("s_s", 1))

                def clip1(e, b=b, width=width):
                    return e.tensor_scalar(wqf[b][:, 0:width], wqf[b][:, 0:width],
                                           MAGIC, 1.0,
                                           op0=AluOpType.subtract, op1=AluOpType.min)
                wsem = "s_w2w" if b == 0 else "s_w2x"
                p.on("vector", clip1, waits=[("s_s", c_sq["s"][gr]),
                                             (wsem, c_sq["g"].get(gr - 2, 0))])
                if idx == 0:
                    c_sq["v"][gr] = p.on("vector", lambda e, r=r, b=b: e.tensor_scalar(
                        w1q_s[:, DH * r:DH * (r + 1)], wqf[b][:], -1.0, None,
                        op0=AluOpType.max), inc=("s_v", 1))
                else:
                    c_sq["v"][gr] = p.on("vector", lambda e, b=b: e.tensor_scalar(
                        wqb[b][:], wqf[b][:, 0:D], -1.0, None, op0=AluOpType.max),
                        inc=("s_v", 1))
                    c_sq["g"][gr] = dma("gpsimd",
                                        w2q_hbm.ap()[128 * r:128 * (r + 1), :],
                                        wqb[b][:], wsem,
                                        waits=[("s_v", c_sq["v"][gr])])
        if debug:
            dma("gpsimd", dbg["d_scalc"], scalc[:, 0:6], "s_gd", waits=[("s_sc", c_scalb)])
            dma("gpsimd", dbg["d_w1q"], w1q_s[:, 0:DH], "s_gd",
                waits=[("s_v", p.now("s_v"))])
            dma("gpsimd", dbg["d_w2q"], w2q_hbm.ap()[0:128, :], "s_gd",
                waits=[("s_w2w", p.now("s_w2w"))])
        c_p0_v = p.now("s_v")
        c_p0_s = p.now("s_s")
        c_p0_w2w = p.now("s_w2w")
        c_p0_w2x = p.now("s_w2x")

    with contextlib.ExitStack() as p1:
        wkv_s = p1.enter_context(nc.sbuf_tensor("wkv_s", [128, KCH * 2 * D], BF16))
        kf16 = [p1.enter_context(nc.sbuf_tensor(f"kf16_{q}", [128, D], BF16)) for q in range(2)]
        vb16 = [p1.enter_context(nc.sbuf_tensor(f"vb16_{q}", [128, D], BF16)) for q in range(2)]
        kvpart = p1.enter_context(nc.sbuf_tensor("kvpart", [128, D], F32))
        scrD = p1.enter_context(nc.sbuf_tensor("scrD1", [128, D], F32))
        ksrow = p1.enter_context(nc.sbuf_tensor("ksrow", [128, 512], F32))
        ps_kvacc = p1.enter_context(nc.psum_tensor("ps_kvacc", [128, D], F32))
        ps_k = p1.enter_context(nc.psum_tensor("ps_k", [128, D], F32))
        ps_v = p1.enter_context(nc.psum_tensor("ps_v", [128, D], F32))
        ps_xT = p1.enter_context(nc.psum_tensor("ps_xT", [128, D], BF16))
        ps_ks = p1.enter_context(nc.psum_tensor("ps_ks", [128, 512], F32))
        p.on("vector", lambda e: e.memset(kvpart[:], 0.0),
             waits=[("s_s", c_p0_s), ("s_w2w", c_p0_w2w), ("s_w2x", c_p0_w2x)])
        p.on("vector", lambda e: e.memset(ksrow[:], 0.0))

        for kk in range(KCH):
            c_wkv = dma("gpsimd", wkv_s[:, 2 * D * kk:2 * D * (kk + 1)],
                        wqkvT[128 * kk:128 * (kk + 1), D:3 * D], "s_wkv",
                        waits=([("s_w2w", c_p0_w2w), ("s_w2x", c_p0_w2x),
                                ("s_s", c_p0_s),
                                ("s_v", c_p0_v)] if kk == 0
                               else [("s_wkv", p.now("s_wkv"))]))

        p1B = p1.enter_context(nc.sbuf_tensor("p1B", [128, D], F32))
        p1C = p1.enter_context(nc.sbuf_tensor("p1C", [128, D], F32))
        p1D = p1.enter_context(nc.sbuf_tensor("p1D", [128, D], F32))
        rot = [scrB, p1B]
        me = [scrC, p1C]
        rl = [scrD, p1D]

        def p1_load(i):
            q = i % 2
            w = ([("s_t", cp.get(("p1_xT", i - 2)))] if i >= 2
                 else [("s_v", c_p0_v), ("s_s", c_p0_s)])
            xsem = "s_g0" if q == 0 else "s_g1"
            cp[("p1_xld", i)] = dma("gpsimd", xb16[q][:],
                                    x_in[128 * i:128 * (i + 1), :], xsem, waits=w)
            cp[("p1_xsem", i)] = xsem

        def p1_frontT(i):
            q = i % 2
            w = [(cp[("p1_xsem", i)], cp[("p1_xld", i)]), ("s_ld", c_consts),
                 ("s_s", cp.get(("p1_xfm_ev", i - 1)))]
            for kk in range(KCH):
                p.on("tensor", lambda e, kk=kk, q=q: e.transpose(
                    ps_xT[:, 128 * kk:128 * (kk + 1)], xb16[q][:, 128 * kk:128 * (kk + 1)],
                    eye_s[:]), waits=w if kk == 0 else ())
            cp[("p1_xT", i)] = p.bump("tensor", "s_t", 1)
            w = [("s_t", cp[("p1_xT", i)]), ("s_t", cp.get(("p1_vmm", i - 2)))]
            cp[("p1_xfm_ev", i)] = p.on(
                "scalar", lambda e, q=q: e.activation(xfm[q][:], ps_xT[:], AF.Copy),
                waits=w, inc=("s_s", 1))

        def p1_kmm(i):
            q = i % 2
            w = [("s_s", cp[("p1_xfm_ev", i)]), ("s_wkv", c_wkv),
                 ("s_v", cp.get(("p1_rope", i - 1)))]
            for ns in range(2):
                for kk in range(KCH):
                    p.on("tensor", lambda e, ns=ns, kk=kk, q=q: e.matmul(
                        ps_k[:, 512 * ns:512 * (ns + 1)], xfm[q][:, 128 * kk:128 * (kk + 1)],
                        wkv_s[:, 2 * D * kk + 512 * ns: 2 * D * kk + 512 * (ns + 1)],
                        start=(kk == 0), stop=(kk == KCH - 1), skip_group_check=True),
                       waits=w if (ns == 0 and kk == 0) else ())
            cp[("p1_kmm", i)] = p.bump("tensor", "s_t", 1)

        def p1_vmm(i):
            q = i % 2
            w = [("s_s", cp.get(("p1_vev", i - 1)))]
            for ns in range(2):
                for kk in range(KCH):
                    p.on("tensor", lambda e, ns=ns, kk=kk, q=q: e.matmul(
                        ps_v[:, 512 * ns:512 * (ns + 1)], xfm[q][:, 128 * kk:128 * (kk + 1)],
                        wkv_s[:, 2 * D * kk + D + 512 * ns: 2 * D * kk + D + 512 * (ns + 1)],
                        start=(kk == 0), stop=(kk == KCH - 1), skip_group_check=True),
                       waits=w if (ns == 0 and kk == 0) else ())
            cp[("p1_vmm", i)] = p.bump("tensor", "s_t", 1)

        def p1_rope(i):
            q = i % 2
            cp[("p1_rope", i)] = p.on(
                "vector", lambda e, i=i, q=q: rope_ops(e, ps_k, rot[q], i, tmp=scrA),
                waits=[("s_t", cp[("p1_kmm", i)])], inc=("s_v", 1))
            cp[("p1_min", i)] = p.on("vector", lambda e, q=q: e.tensor_scalar(
                me[q][:], rot[q][:], 0.0, None, op0=AluOpType.min), inc=("s_v", 1))

        def p1_feat(i):
            q = i % 2
            c_exp = p.on("scalar", lambda e, q=q: e.activation(me[q][:], me[q][:], AF.Exp),
                         waits=[("s_v", cp[("p1_min", i)])], inc=("s_s", 1))
            c_rel = p.on("scalar", lambda e, q=q: e.activation(rl[q][:], rot[q][:], AF.Relu),
                         inc=("s_s", 1))
            cp[("p1_kf", i)] = p.on("vector", lambda e, q=q: e.tensor_tensor(
                kf16[q][:], me[q][:], rl[q][:], op=AluOpType.add),
                waits=[("s_s", c_rel), ("s_t", cp.get(("p1_kv", i - 2)))], inc=("s_v", 1))
            cp[("p1_vev", i)] = p.on("scalar", lambda e, q=q: e.activation(
                vb16[q][:], ps_v[:], AF.Copy),
                waits=[("s_t", cp[("p1_vmm", i)]), ("s_t", cp.get(("p1_kv", i - 2)))],
                inc=("s_s", 1))

        def p1_acc(i):
            q = i % 2
            w = [("s_v", cp[("p1_kf", i)]), ("s_s", cp[("p1_vev", i)]),
                 ("s_v", c_onesb), ("s_v", cp.get(("p1_kvadd", i - 1)))]
            for c8 in range(KCH):
                p.on("tensor", lambda e, c8=c8, q=q: e.matmul(
                    ps_kvacc[:, 128 * c8:128 * (c8 + 1)], kf16[q][:, 128 * c8:128 * (c8 + 1)],
                    vb16[q][:, 128 * c8:128 * (c8 + 1)],
                    start=True, stop=True, skip_group_check=True),
                   waits=w if c8 == 0 else ())
            for ns, base in ((0, 0), (1, 32)):
                p.on("tensor", lambda e, ns=ns, base=base, q=q: e.matmul(
                    ps_ks[base:base + 1, 0:512], onesb_s[:],
                    kf16[q][:, 512 * ns:512 * (ns + 1)],
                    start=True, stop=True, skip_group_check=True,
                    tile_position=(0, base)))
            cp[("p1_kv", i)] = p.bump("tensor", "s_t", 1)

            def kvadd(e):
                e.tensor_tensor(kvpart[:], kvpart[:], ps_kvacc[:], op=AluOpType.add)
                e.tensor_tensor(ksrow[0:1, :], ksrow[0:1, :], ps_ks[0:1, :],
                                op=AluOpType.add)
                e.tensor_tensor(ksrow[32:33, :], ksrow[32:33, :], ps_ks[32:33, :],
                                op=AluOpType.add)
                return e.drain()
            cp[("p1_kvadd", i)] = p.on("vector", kvadd,
                                       waits=[("s_t", cp[("p1_kv", i)])],
                                       inc=("s_v", 1))

        p1_load(0)
        p1_frontT(0)
        p1_kmm(0)
        p1_vmm(0)
        for i in range(NT):
            p1_rope(i)
            if i >= 1:
                p1_acc(i - 1)
            if i + 1 < NT:
                p1_load(i + 1)
                p1_frontT(i + 1)
                p1_kmm(i + 1)
            p1_feat(i)
            if i + 1 < NT:
                p1_vmm(i + 1)
        p1_acc(NT - 1)
        c_p1_t = p.now("s_t")
        ck1 = cp[("p1_kvadd", NT - 1)]
        dma("gpsimd", cc_in.ap()[0:128, :], kvpart[:], "s_ci", waits=[("s_v", ck1)])
        dma("gpsimd", cc_in.ap()[128:129, 0:512], ksrow[0:1, :], "s_ci",
            waits=[("s_ci", p.now("s_ci"))])
        c_ccin = dma("gpsimd", cc_in.ap()[128:129, 512:1024], ksrow[32:33, :], "s_ci",
                     waits=[("s_ci", p.now("s_ci"))])
        c_cc = p.on("gpsimd", lambda e: e.collective_compute(
            "AllReduce", mybir.AluOpType.add,
            replica_groups=[[0, 1, 2, 3], [4, 5, 6, 7]],
            ins=[cc_in.ap().opt()], outs=[cc_out.ap().opt()]),
            waits=[("s_ci", c_ccin)], inc=("s_cc", 1))
        c_kvrd = dma("gpsimd", kvpart[:], cc_out.ap()[0:128, :], "s_kvr",
                     waits=[("s_cc", c_cc)])
        c_ksb = dma("gpsimd", ksum_b[:], cc_out.ap()[128:129, :].broadcast_to([128, D]),
                    "s_ksb")
        if debug:
            dma("gpsimd", dbg["d_ccin"], cc_in.ap(), "s_gd", waits=[("s_ksb", c_ksb)])
            dma("gpsimd", dbg["d_ccout"], cc_out.ap(), "s_gd")
            dma("gpsimd", dbg["d_kv"], kvpart[:], "s_gd")
            dma("gpsimd", dbg["d_ks"], ksum_b[:], "s_gd")
        p.on("vector", lambda e: e.memset(kvaug[:], 0.0), waits=[("s_kvr", c_kvrd)])
        for c8 in range(KCH):
            def cpk(e, c8=c8):
                e.tensor_copy(kvaug[0:64, 128 * c8:128 * c8 + 64],
                              kvpart[0:64, 128 * c8:128 * c8 + 64])
                return e.tensor_copy(kvaug[64:128, 128 * c8 + 64:128 * (c8 + 1)],
                                     kvpart[64:128, 128 * c8 + 64:128 * (c8 + 1)])
            p.on("vector", cpk)
        c_kvaug = p.bump("vector", "s_v", 1, fence=True)

    with contextlib.ExitStack() as p2:
        xb32 = [p2.enter_context(nc.sbuf_tensor(f"xb32_{q}", [128, D], F32)) for q in range(2)]
        x1b = [p2.enter_context(nc.sbuf_tensor(f"x1b_{q}", [128, D], F32)) for q in range(2)]
        h_act = p2.enter_context(nc.sbuf_tensor("h_act", [128, DH], F32))
        hqfm = [p2.enter_context(nc.sbuf_tensor(f"hqfm_{q}", [128, DH], BF16)) for q in range(2)]
        w2b = [p2.enter_context(nc.sbuf_tensor(f"w2b_{b}", [128, D], BF16)) for b in range(4)]
        yA = p2.enter_context(nc.sbuf_tensor("yA", [128, D], F32))
        yB = p2.enter_context(nc.sbuf_tensor("yB", [128, D], F32))
        bs1 = p2.enter_context(nc.sbuf_tensor("bs1", [128, D], BF16))
        bs2 = p2.enter_context(nc.sbuf_tensor("bs2", [128, D], BF16))
        bs3 = p2.enter_context(nc.sbuf_tensor("bs3", [128, D], BF16))
        bs4 = p2.enter_context(nc.sbuf_tensor("bs4", [128, D], BF16))
        bs5 = p2.enter_context(nc.sbuf_tensor("bs5", [128, D], BF16))
        bs6 = p2.enter_context(nc.sbuf_tensor("bs6", [128, D], BF16))
        ps_T = p2.enter_context(nc.psum_tensor("ps_T", [128, 2048], BF16))
        ps_q = p2.enter_context(nc.psum_tensor("ps_q", [128, D], F32))
        ps_at = p2.enter_context(nc.psum_tensor("ps_at", [128, D], F32))
        ps_f = p2.enter_context(nc.psum_tensor("ps_f", [128, D], F32))
        DUMP = h_act[:, 2048:3072]

        def tA_u(i):
            q = i % 2
            w = [("s_t", cp[("p2_womm", i)]), ("s_s", cp.get(("p2_s2", i - 1)))]
            c_u = p.on("vector", lambda e, q=q: e.tensor_tensor(
                scrC[:], ps_q[:], xb32[q][:], op=AluOpType.add),
                waits=w, inc=("s_v", 1))
            if i + 2 < n_tiles:
                xsemn = "s_xa" if i % 2 == 0 else "s_xb"
                cp[("p2_xld", i + 2)] = dma(
                    "scalar", xb32[i % 2][:], x_in[128 * (i + 2):128 * (i + 3), :],
                    xsemn, waits=[("s_v", c_u), ("s_ksb", c_ksb)])
            cp[("p2_u", i)] = c_u
            cp[("p2_psq_free", i)] = c_u
            if debug and i == DBG_TILE:
                cdbg = dma("gpsimd", dbg["d_u"], scrC[:], "s_gd", waits=[("s_v", c_u)])
                cdbg2 = dma("gpsimd", dbg["d_att"], bs5[:], "s_gd")
                c_u = p.on("vector", lambda e: e.memset(cols[:, 30:31], 0.0),
                           waits=[("s_gd", cdbg2)], inc=("s_v", 1))
                cp[("p2_u", i)] = c_u
                cp[("p2_psq_free", i)] = c_u

        def tA_front(i):
            q = i % 2
            xsem = "s_xa" if q == 0 else "s_xb"
            c_xld = cp.pop(("p2_xld", i), None)
            if c_xld is None:
                w = [("s_v", cp.get(("p2_u", i - 2))), ("s_ksb", c_ksb)]
                c_xld = dma("scalar", xb32[q][:], x_in[128 * i:128 * (i + 1), :],
                            xsem, waits=w)
            w = [(xsem, c_xld), ("s_t", cp.get(("p2_xT", i - 2)))]
            c_xb = p.on("vector", lambda e, q=q: e.tensor_copy(xb16[q][:], xb32[q][:]),
                        waits=w, inc=("s_v", 1))
            w = [("s_v", c_xb), ("s_s", cp.get(("p2_xfm_ev", i - 1))),
                 ("s_v", c_kvaug)]
            for kk in range(KCH):
                p.on("tensor", lambda e, kk=kk, q=q: e.transpose(
                    ps_T[:, 128 * kk:128 * (kk + 1)], xb16[q][:, 128 * kk:128 * (kk + 1)],
                    eye_s[:]), waits=w if kk == 0 else ())
            cp[("p2_xT", i)] = p.bump("tensor", "s_t", 1)
            w = [("s_t", cp[("p2_xT", i)]), ("s_t", cp.get(("p2_qmm", i - 2)))]
            cp[("p2_xfm_ev", i)] = p.on(
                "scalar", lambda e, q=q: e.activation(xfm[q][:], ps_T[:, 0:1024], AF.Copy),
                waits=w, inc=("s_s", 1))
            w = [("s_s", cp[("p2_xfm_ev", i)]), ("s_wc", c_wcast),
                 ("s_v", cp.get(("p2_psq_free", i - 1)))]
            for ns in range(2):
                for kk in range(KCH):
                    p.on("tensor", lambda e, ns=ns, kk=kk, q=q: e.matmul(
                        ps_q[:, 512 * ns:512 * (ns + 1)], xfm[q][:, 128 * kk:128 * (kk + 1)],
                        wq_s[:, D * kk + 512 * ns: D * kk + 512 * (ns + 1)],
                        start=(kk == 0), stop=(kk == KCH - 1), skip_group_check=True),
                       waits=w if (ns == 0 and kk == 0) else ())
            cp[("p2_qmm", i)] = p.bump("tensor", "s_t", 1)

        def tA_mid1(i):
            w = [("s_t", cp[("p2_qmm", i)])]
            c_rope = p.on("vector", lambda e, i=i: rope_ops(e, ps_q, scrB, i, tmp=yA),
                          waits=w, inc=("s_v", 1))
            c_min = p.on("vector", lambda e: e.tensor_scalar(
                yB[:], scrB[:], 0.125, 0.0, op0=AluOpType.mult, op1=AluOpType.min),
                inc=("s_v", 1))
            c_exp = p.on("scalar", lambda e: e.activation(yB[:], yB[:], AF.Exp),
                         waits=[("s_v", c_min)], inc=("s_s", 1))
            c_rel = p.on("scalar", lambda e: e.activation(
                yA[:], scrB[:], AF.Relu, scale=0.125), inc=("s_s", 1))
            c_qf = p.on("vector", lambda e: e.tensor_tensor(
                scrB[:], yB[:], yA[:], op=AluOpType.add),
                waits=[("s_s", c_rel)], inc=("s_v", 1))
            cp[("p2_qf", i)] = c_qf
            if debug and i == DBG_TILE:
                cdbg = dma("gpsimd", dbg["d_qf"], scrB[:], "s_gd", waits=[("s_v", c_qf)])
                c_qf = p.on("vector", lambda e: e.memset(cols[:, 30:31], 0.0),
                            waits=[("s_gd", cdbg)], inc=("s_v", 1))
                cp[("p2_qf", i)] = c_qf

            def denom(e):
                e.tensor_tensor(yA[:], scrB[:], ksum_b[:], op=AluOpType.mult)
                e.tensor_reduce(cols[:, 0:16],
                                yA[:].rearrange("t (h f) -> t h f", h=H), axis=AX.X,
                                op=AluOpType.add)
                e.drain()
                e.tensor_scalar(cols[:, 0:16], cols[:, 0:16], 1e-6, None,
                                op0=AluOpType.max)
                e.drain()
                e.reciprocal(cols[:, 16:32], cols[:, 0:16])
                return e.drain()
            p.on("vector", denom, waits=[("s_ksb", c_ksb)])
            w = [("s_t", cp.get(("p2_qfsT", i - 1)))]
            for h in range(H):
                p.on("vector", lambda e, h=h: e.tensor_scalar(
                    bs1[:, 64 * h:64 * (h + 1)], scrB[:, 64 * h:64 * (h + 1)],
                    cols[:, 16 + h:17 + h], None, op0=AluOpType.mult),
                   waits=w if h == 0 else ())
            cp[("p2_qfs", i)] = p.bump("vector", "s_v", 1, fence=True)

        def tA_midT(i):
            w = [("s_v", cp[("p2_qfs", i)]), ("s_s", cp.get("psTqB_ev"))]
            for kk in range(KCH):
                p.on("tensor", lambda e, kk=kk: e.transpose(
                    ps_T[:, 1024 + 128 * kk:1024 + 128 * (kk + 1)],
                    bs1[:, 128 * kk:128 * (kk + 1)], eye_s[:]),
                   waits=w if kk == 0 else ())
            cp[("p2_qfsT", i)] = p.bump("tensor", "s_t", 1)
            w = [("s_t", cp[("p2_qfsT", i)]), ("s_t", cp.get(("p2_atmm", i - 1)))]
            c_qff = p.on("scalar", lambda e: e.activation(bs4[:], ps_T[:, 1024:2048], AF.Copy),
                         waits=w, inc=("s_s", 1))
            cp["psTqB_ev"] = c_qff
            w = [("s_s", c_qff), ("s_v", c_kvaug),
                 ("s_v", cp.get("psat_free")), ("s_s", cp.get(("p2_ate", i - 1)))]
            for c8 in range(KCH):
                p.on("tensor", lambda e, c8=c8: e.matmul(
                    ps_at[:, 128 * c8:128 * (c8 + 1)], kvaug[:, 128 * c8:128 * (c8 + 1)],
                    bs4[:, 128 * c8:128 * (c8 + 1)], start=True, stop=True,
                    skip_group_check=True), waits=w if c8 == 0 else ())
            cp[("p2_atmm", i)] = p.bump("tensor", "s_t", 1)
            w = [("s_t", cp[("p2_atmm", i)]), ("s_t", cp.get(("p2_womm", i - 1)))]
            c_ate = p.on("scalar", lambda e: e.activation(bs5[:], ps_at[:], AF.Copy),
                         waits=w, inc=("s_s", 1))
            cp[("p2_ate", i)] = c_ate
            w = [("s_s", c_ate), ("s_v", cp[("p2_qf", i)])]
            for ns in range(2):
                for kk in range(KCH):
                    p.on("tensor", lambda e, ns=ns, kk=kk: e.matmul(
                        ps_q[:, 512 * ns:512 * (ns + 1)], bs5[:, 128 * kk:128 * (kk + 1)],
                        wout_s[:, D * kk + 512 * ns: D * kk + 512 * (ns + 1)],
                        start=(kk == 0), stop=(kk == KCH - 1), skip_group_check=True),
                       waits=w if (ns == 0 and kk == 0) else ())
            cp[("p2_womm", i)] = p.bump("tensor", "s_t", 1)

        def tA_backA(i):
            q = i % 2
            s = i % 2
            c_u = cp[("p2_u", i)]
            p.on("scalar", lambda e: e.activation(
                DUMP, scrC[:], AF.Identity, accum_out=stats[:, 0:1]),
                waits=[("s_v", c_u)], inc=("s_s", 1))
            c_s2 = p.on("scalar", lambda e: e.activation(
                DUMP, scrC[:], AF.Square, accum_out=stats[:, 1:2]),
                inc=("s_s", 1), fence=True)
            cp[("p2_s2", i)] = c_s2

            def ln1cols(e):
                e.tensor_scalar(stats[:, 2:3], stats[:, 0:1], 1.0 / D, None,
                                op0=AluOpType.mult)
                e.tensor_tensor(stats[:, 3:4], stats[:, 2:3], stats[:, 2:3],
                                op=AluOpType.mult)
                e.tensor_scalar(stats[:, 4:5], stats[:, 1:2], 1.0 / D, None,
                                op0=AluOpType.mult)
                e.tensor_tensor(stats[:, 4:5], stats[:, 4:5], stats[:, 3:4],
                                op=AluOpType.subtract)
                e.tensor_scalar(stats[:, 3:4], stats[:, 4:5], EPS, None,
                                op0=AluOpType.add)
                e.tensor_tensor(stats[:, 5:6], stats[:, 3:4], ones_s[:, 1:2],
                                op=AluOpType.pow)
                return e.tensor_tensor(stats[:, 18:19], stats[:, 3:4], ones_s[:, 1:2],
                                       op=AluOpType.pow)
            c_rs = p.on("gpsimd", ln1cols, waits=[("s_s", c_s2)], inc=("s_gc", 1))
            c_x1 = p.on("vector", lambda e, s=s: e.tensor_scalar(
                x1b[s][:], scrC[:], stats[:, 2:3], stats[:, 5:6],
                op0=AluOpType.subtract, op1=AluOpType.mult),
                waits=[("s_gc", c_rs)], inc=("s_v", 1))

            c_am1 = p.on("vector", lambda e, s=s: e.tensor_reduce(
                cols[:, 32:33], x1b[s][:], axis=AX.X, op=AluOpType.max,
                apply_absolute_value=True),
                waits=[("s_v", c_x1)], inc=("s_v", 1), fence=True)

            def q1cols(e, s=s):
                e.tensor_scalar(cols[:, 32:33], cols[:, 32:33], EPS, None,
                                op0=AluOpType.max)
                e.tensor_tensor(cols[:, 33:34], cols[:, 32:33], ones_s[:, 3:4],
                                op=AluOpType.pow)
                e.tensor_scalar(cols[:, 33:34], cols[:, 33:34], 127.0, None,
                                op0=AluOpType.mult)
                e.tensor_tensor(cols[:, 34 + s:35 + s], cols[:, 32:33], scalc[:, 1:2],
                                op=AluOpType.mult)
                return e.tensor_tensor(cols[:, 30:31], cols[:, 32:33], scalc[:, 1:2],
                                       op=AluOpType.mult)
            c_q1g = p.on("gpsimd", q1cols, waits=[("s_v", c_am1)], inc=("s_gc", 1))

            def q1(e, s=s):
                e.tensor_scalar(scrA[:], x1b[s][:], cols[:, 33:34], MAGIC,
                                op0=AluOpType.mult, op1=AluOpType.add)
                e.tensor_scalar(scrA[:], scrA[:], MAGIC, 127.0,
                                op0=AluOpType.subtract, op1=AluOpType.min)
                return e.tensor_scalar(bs2[:], scrA[:], -128.0, None, op0=AluOpType.max)
            if debug and i == DBG_TILE:
                dma("gpsimd", dbg["d_x1"], x1b[s][:], "s_gd", waits=[("s_v", c_x1)])
            c_q1 = p.on("vector", q1,
                        waits=[("s_gc", c_q1g), ("s_t", cp.get(("p2_xq1T", i - 1)))],
                        inc=("s_v", 1))
            cp[("p2_q1", i)] = c_q1
            w = [("s_v", c_q1), ("s_s", cp.get("psTqB_ev"))]
            for kk in range(KCH):
                p.on("tensor", lambda e, kk=kk: e.transpose(
                    ps_T[:, 1024 + 128 * kk:1024 + 128 * (kk + 1)],
                    bs2[:, 128 * kk:128 * (kk + 1)], eye_s[:]),
                   waits=w if kk == 0 else ())
            cp[("p2_xq1T", i)] = p.bump("tensor", "s_t", 1)
            w = [("s_t", cp[("p2_xq1T", i)]), ("s_t", cp.get(("p2_f1mm", i - 1)))]
            c_xqf = p.on("scalar", lambda e: e.activation(bs6[:], ps_T[:, 1024:2048], AF.Copy),
                         waits=w, inc=("s_s", 1))
            cp["psTqB_ev"] = c_xqf

            gcp = []
            for nsl in range(8):
                half = ps_f[:, 512 * (nsl % 2):512 * (nsl % 2 + 1)]
                w = [("s_s", c_xqf), ("s_v", c_p0_v)]
                if nsl >= 2:
                    w.append(("s_s", gcp[nsl - 2]))
                else:
                    w.append(("s_v", cp.get("psf_free")))
                for kk in range(KCH):
                    p.on("tensor", lambda e, half=half, nsl=nsl, kk=kk: e.matmul(
                        half, bs6[:, 128 * kk:128 * (kk + 1)],
                        w1q_s[:, DH * kk + 512 * nsl: DH * kk + 512 * (nsl + 1)],
                        start=(kk == 0), stop=(kk == KCH - 1), skip_group_check=True),
                       waits=w if kk == 0 else ())
                c_mm = p.bump("tensor", "s_t", 1)
                if nsl == 7:
                    cp[("p2_f1mm", i)] = c_mm
                gw = [("s_t", c_mm), ("s_gc", c_rs)]
                if nsl < 2:
                    gw.append(("s_oa" if nsl == 0 else "s_ob",
                               cp.get(("p2_out", i // 2 - 1, nsl))))
                cg = p.on("scalar", lambda e, half=half, nsl=nsl, s=s: e.activation(
                    h_act[:, 512 * nsl:512 * (nsl + 1)], half, AF.Gelu,
                    scale=cols[:, 34 + s:35 + s]),
                    waits=gw, inc=("s_s", 1))
                gcp.append(cg)
            cp[("p2_gl", i)] = gcp[-1]


        def tA_backB(i):
            q = i % 2
            s = i % 2
            def q2(e, s=s):
                return e.tensor_reduce(cols[:, 36:37], h_act[:], axis=AX.X,
                                       op=AluOpType.max, apply_absolute_value=True)

            def q2cols(e, s=s):
                e.tensor_scalar(cols[:, 36:37], cols[:, 36:37], EPS, None,
                                op0=AluOpType.max)
                e.tensor_tensor(cols[:, 37:38], cols[:, 36:37], ones_s[:, 3:4],
                                op=AluOpType.pow)
                e.tensor_scalar(cols[:, 37:38], cols[:, 37:38], 127.0, None,
                                op0=AluOpType.mult)
                e.tensor_tensor(cols[:, 38 + s:39 + s], cols[:, 36:37],
                                scalc[:, 3:4], op=AluOpType.mult)
                return e.tensor_tensor(cols[:, 30:31], cols[:, 36:37],
                                       scalc[:, 3:4], op=AluOpType.mult)
            if debug and i == DBG_TILE:
                cdbg = dma("gpsimd", dbg["d_xq1"], bs2[:], "s_gd",
                           waits=[("s_v", cp[("p2_q1", i)])])
                cdbg2 = dma("gpsimd", dbg["d_hact"], h_act[:], "s_gd",
                            waits=[("s_s", cp[("p2_gl", i)])])
            c_q2 = p.on("vector", q2, waits=[("s_s", cp[("p2_gl", i)])], inc=("s_v", 1),
                        fence=True)
            c_q2 = p.on("gpsimd", q2cols, waits=[("s_v", c_q2)], inc=("s_gc", 1))
            if debug and i == DBG_TILE:
                c_q2 = p.on("vector", lambda e: e.memset(cols[:, 30:31], 0.0),
                            waits=[("s_gd", cdbg2)], inc=("s_v", 1))
            prev_T = cp.get(("p2_hqT", i - 1))
            for nsl in range(4):
                def qs(e, nsl=nsl):
                    e.tensor_scalar(scrA[:], h_act[:, 1024 * nsl:1024 * (nsl + 1)],
                                    cols[:, 37:38], MAGIC, op0=AluOpType.mult,
                                    op1=AluOpType.add)
                    e.tensor_scalar(scrA[:], scrA[:], MAGIC, 127.0,
                                    op0=AluOpType.subtract, op1=AluOpType.min)
                    return e.tensor_scalar(bs3[:], scrA[:], -128.0, None,
                                           op0=AluOpType.max)
                w = [("s_gc", c_q2), ("s_t", prev_T)]
                c_qs = p.on("vector", qs, waits=w, inc=("s_v", 1))
                w = [("s_v", c_qs), ("s_s", cp.get("psTqB_ev"))]
                for kk in range(KCH):
                    p.on("tensor", lambda e, kk=kk: e.transpose(
                        ps_T[:, 1024 + 128 * kk:1024 + 128 * (kk + 1)],
                        bs3[:, 128 * kk:128 * (kk + 1)], eye_s[:]),
                       waits=w if kk == 0 else ())
                prev_T = p.bump("tensor", "s_t", 1)
                cp["psTqB_ev"] = p.on(
                    "scalar", lambda e, s=s, nsl=nsl: e.activation(
                        hqfm[s][:, 1024 * nsl:1024 * (nsl + 1)], ps_T[:, 1024:2048],
                        AF.Copy), waits=[("s_t", prev_T)], inc=("s_s", 1))
            cp[("p2_hqT", i)] = prev_T
            cp[("p2_hq_ready", i)] = p.now("s_s")
            if debug and i == DBG_TILE:
                dma("gpsimd", dbg["d_hq"], hqfm[DBG_TILE % 2][:], "s_gd",
                    waits=[("s_s", cp[("p2_hq_ready", i)])])

        def pair_B(j):
            i0, i1 = 2 * j, 2 * j + 1
            for kk in range(32):
                b = kk % 4
                if kk >= 4:
                    w = [("s_t", cp.get(("p2_w2u", kk - 4, j)))]
                elif j >= 1:
                    w = [("s_t", cp.get(("p2_w2u", 28 + kk, j - 1)))]
                else:
                    w = [("s_ksb", c_ksb)]
                wsem = f"s_w2{b}"
                c_wl = dma("sync", w2b[b][:], w2q_hbm.ap()[128 * kk:128 * (kk + 1), :],
                           wsem, waits=w)
                w = [(wsem, c_wl), ("s_s", cp[("p2_hq_ready", i1)])]
                if kk == 0:
                    w.append(("s_s", cp.get(("p2_ate", i1 + 1))))
                first = True
                for s, tgt in ((0, ps_f), (1, ps_at)):
                    for nh in range(2):
                        p.on("tensor", lambda e, tgt=tgt, s=s, kk=kk, nh=nh: e.matmul(
                            tgt[:, 512 * nh:512 * (nh + 1)],
                            hqfm[s][:, 128 * kk:128 * (kk + 1)],
                            w2b[kk % 4][:, 512 * nh:512 * (nh + 1)],
                            start=(kk == 0), stop=(kk == 31), skip_group_check=True),
                           waits=w if first else ())
                        first = False
                cp[("p2_w2u", kk, j)] = p.bump("tensor", "s_t", 1)
            c_f2 = p.now("s_t")
            cp[("p2_f2mm", i0)] = c_f2
            cp[("p2_f2mm", i1)] = c_f2
            for s, i, tgt in ((0, i0, ps_f), (1, i1, ps_at)):
                b2 = 6 + 6 * s
                ofm = h_act[:, 1024 * s:1024 * (s + 1)]
                osem = "s_oa" if s == 0 else "s_ob"
                w = [("s_t", c_f2), (osem, cp.get(("p2_out", j - 1, s)))]

                def evac(e, s=s, tgt=tgt):
                    return e.scalar_tensor_tensor(scrC[:], tgt[:], cols[:, 38 + s:39 + s],
                                                  x1b[s][:], op0=AluOpType.mult,
                                                  op1=AluOpType.add)
                c_ev = p.on("vector", evac, waits=w, inc=("s_v", 1))
                if s == 0:
                    cp["psf_free"] = c_ev
                else:
                    cp["psat_free"] = c_ev
                if debug and j == DBG_TILE // 2 and s == DBG_TILE % 2:
                    cdbg = dma("gpsimd", dbg["d_u2"], scrC[:], "s_gd", waits=[("s_v", c_ev)])
                    c_ev = p.on("vector", lambda e: e.memset(cols[:, 30:31], 0.0),
                                waits=[("s_gd", cdbg)], inc=("s_v", 1))
                p.on("scalar", lambda e, b2=b2: e.activation(
                    DUMP, scrC[:], AF.Identity, accum_out=stats[:, b2:b2 + 1]),
                    waits=[("s_v", c_ev)], inc=("s_s", 1))
                c_es2 = p.on("scalar", lambda e, b2=b2: e.activation(
                    DUMP, scrC[:], AF.Square, accum_out=stats[:, b2 + 1:b2 + 2]),
                    inc=("s_s", 1), fence=True)
                if s == 1:
                    cp[("p2_s2", i)] = c_es2

                def ln2cols(e, b2=b2):
                    e.tensor_scalar(stats[:, b2 + 2:b2 + 3], stats[:, b2:b2 + 1],
                                    1.0 / D, None, op0=AluOpType.mult)
                    e.tensor_tensor(stats[:, b2 + 4:b2 + 5], stats[:, b2 + 2:b2 + 3],
                                    stats[:, b2 + 2:b2 + 3], op=AluOpType.mult)
                    e.tensor_scalar(stats[:, b2 + 3:b2 + 4], stats[:, b2 + 1:b2 + 2],
                                    1.0 / D, None, op0=AluOpType.mult)
                    e.tensor_tensor(stats[:, b2 + 3:b2 + 4], stats[:, b2 + 3:b2 + 4],
                                    stats[:, b2 + 4:b2 + 5], op=AluOpType.subtract)
                    e.tensor_scalar(stats[:, b2 + 3:b2 + 4],
                                    stats[:, b2 + 3:b2 + 4], EPS, None,
                                    op0=AluOpType.add)
                    e.tensor_tensor(stats[:, b2 + 5:b2 + 6], stats[:, b2 + 3:b2 + 4],
                                    ones_s[:, 1:2], op=AluOpType.pow)
                    return e.tensor_tensor(stats[:, b2 + 4:b2 + 5],
                                           stats[:, b2 + 3:b2 + 4],
                                           ones_s[:, 1:2], op=AluOpType.pow)
                c_rs2 = p.on("gpsimd", ln2cols, waits=[("s_s", c_es2)],
                             inc=("s_gc", 1))

                def fin(e, b2=b2, ofm=ofm):
                    return e.tensor_scalar(ofm, scrC[:], stats[:, b2 + 2:b2 + 3],
                                           stats[:, b2 + 5:b2 + 6],
                                           op0=AluOpType.subtract, op1=AluOpType.mult)
                c_fin = p.on("vector", fin,
                             waits=[("s_gc", c_rs2)], inc=("s_v", 1), fence=True)
                cp[("p2_out", j, s)] = dma(
                    "scalar", y_out[128 * i:128 * (i + 1), :], ofm, osem,
                    waits=[("s_v", c_fin)])
                if dual_store:
                    cp[("p2_out", j, s)] = dma(
                        "scalar", y2_out[128 * i:128 * (i + 1), :], ofm, osem,
                        waits=[(osem, p.now(osem))])

        tA_front(0)
        tA_mid1(0)
        tA_midT(0)
        for i in range(NT):
            tA_u(i)
            if i + 1 < NT:
                tA_front(i + 1)
            tA_backA(i)
            if i + 1 < NT:
                tA_mid1(i + 1)
                tA_midT(i + 1)
            tA_backB(i)
            if i % 2 == 1:
                pair_B(i // 2)

        p.on("scalar", lambda e: None,
             waits=[("s_oa", p.now("s_oa")), ("s_ob", p.now("s_ob"))])

    sem_names = set()
    for eng in p.ENGINES:
        for waits, fn, inc, fence in p.items[eng]:
            for sname, v in waits:
                sem_names.add(sname)
            if inc:
                sem_names.add(inc[0])
    with contextlib.ExitStack() as semstack:
        sems = {sn: semstack.enter_context(nc.semaphore(sn)) for sn in sorted(sem_names)}
        with nc.Block() as block:
            p.emit(block, sems)
    es.close()
    return nc


_ROPE_BASE = 10000.0


def _tables(n_tiles, pos0):
    t = np.arange(pos0, pos0 + n_tiles * 128, dtype=np.float32)
    inv = (1.0 / _ROPE_BASE ** (np.arange(0, HD, 2, dtype=np.float32) / HD)).astype(np.float32)
    fr = (t[:, None] * inv[None, :]).astype(np.float32)
    cos = np.cos(fr.astype(np.float64)).astype(np.float32)
    sin = np.sin(fr.astype(np.float64)).astype(np.float32)

    def tile(a):
        return np.ascontiguousarray(
            a.reshape(n_tiles, 128, 32).transpose(1, 0, 2).reshape(128, n_tiles * 32))
    return tile(cos), tile(sin)


def kernel(L0_attn_x, L0_attn_w_qkv, L0_attn_w_out, L0_ffn_w1, L0_ffn_w2):
    B, T, _ = L0_attn_x.shape
    tokens_per_core = B * T // NCORES
    n_tiles = tokens_per_core // 128
    cores_per_batch = NCORES // B

    x_flat = np.ascontiguousarray(np.asarray(L0_attn_x, dtype=np.float32).reshape(B * T, D))
    wqkvT = np.ascontiguousarray(np.asarray(L0_attn_w_qkv, dtype=np.float32).T)
    woutT = np.ascontiguousarray(np.asarray(L0_attn_w_out, dtype=np.float32).T)
    w1T = np.ascontiguousarray(np.asarray(L0_ffn_w1, dtype=np.float32).T)
    w2T = np.ascontiguousarray(np.asarray(L0_ffn_w2, dtype=np.float32).T)
    eye = np.eye(128, dtype=ml_dtypes.bfloat16)
    onesc = np.ascontiguousarray(
        np.stack([np.ones(128, np.float32), np.full(128, -0.5, np.float32),
                  np.full(128, 127.0, np.float32), np.full(128, -1.0, np.float32)],
                 axis=1))

    nc = build(n_tiles)
    in_maps = []
    for c in range(NCORES):
        pos0 = (c % cores_per_batch) * tokens_per_core
        ct, st = _tables(n_tiles, pos0)
        in_maps.append({
            "x": x_flat[c * tokens_per_core:(c + 1) * tokens_per_core],
            "wqkvT": wqkvT, "woutT": woutT, "w1T": w1T, "w2T": w2T,
            "ctab": ct, "stab": st, "eye": eye, "onesc": onesc,
        })
    res = run_bass_kernel_spmd(nc, in_maps, core_ids=list(range(NCORES)))
    out = np.concatenate([np.asarray(r["y"]) for r in res.results], axis=0)
    return out.reshape(B, T, D).astype(np.float32)


# version 19
# speedup vs baseline: 1.1410x; 1.1410x over previous
import contextlib
import numpy as np
import ml_dtypes

import concourse.bass as bass
import concourse.mybir as mybir
from concourse.alu_op_type import AluOpType
from concourse.bass_utils import run_bass_kernel_spmd

F32 = mybir.dt.float32
BF16 = mybir.dt.bfloat16
AF = mybir.ActivationFunctionType
AX = mybir.AxisListType

D = 1024
H = 16
HD = 64
DH = 4096
KCH = 8
MAGIC = 3 * 2.0 ** 22
EPS = 1e-5
NCORES = 8


class Prog:

    ENGINES = ("sync", "tensor", "vector", "scalar", "gpsimd")

    def __init__(self):
        self.items = {e: [] for e in self.ENGINES}
        self.cnt = {}
        self.hw = {e: {} for e in self.ENGINES}

    def now(self, sem):
        return self.cnt.get(sem, 0)

    def on(self, eng, fn, waits=(), inc=None, fence=False):
        w = []
        for sem, val in waits:
            if val is None or val <= 0:
                continue
            if val > self.hw[eng].get(sem, 0):
                self.hw[eng][sem] = val
                w.append((sem, val))
        self.items[eng].append([w, fn, None, fence])
        if inc is not None:
            return self.bump(eng, *inc)
        return None

    def bump(self, eng, sem, k, fence=False):
        assert self.items[eng][-1][2] is None
        self.items[eng][-1][2] = (sem, k)
        if fence:
            self.items[eng][-1][3] = True
        self.cnt[sem] = self.cnt.get(sem, 0) + k
        return self.cnt[sem]

    def emit(self, block, sems):
        for name in self.ENGINES:
            eng_items = self.items[name]

            def make(eng_items=eng_items):
                def body(e):
                    for waits, fn, inc, fence in eng_items:
                        for sem, val in waits:
                            e.wait_ge(sems[sem], val)
                        ins = fn(e)
                        if fence:
                            ins = e.drain()
                        if inc is not None:
                            assert ins is not None, "inc on op returning None"
                            ins.then_inc(sems[inc[0]], inc[1])
                return body

            getattr(block, name)(make())


DBG_TILE = 8


def build(n_tiles, debug=False, dual_store=False):
    assert n_tiles % 2 == 0
    NT = n_tiles
    nc = bass.Bass("TRN2", target_bir_lowering=False, detect_race_conditions=False)

    x_in = nc.dram_tensor("x", [NT * 128, D], F32, kind="ExternalInput").ap()
    wqkvT = nc.dram_tensor("wqkvT", [D, 3 * D], F32, kind="ExternalInput").ap()
    woutT = nc.dram_tensor("woutT", [D, D], F32, kind="ExternalInput").ap()
    w1T = nc.dram_tensor("w1T", [D, DH], F32, kind="ExternalInput").ap()
    w2T = nc.dram_tensor("w2T", [DH, D], F32, kind="ExternalInput").ap()
    ct_in = nc.dram_tensor("ctab", [128, NT * 32], F32, kind="ExternalInput").ap()
    st_in = nc.dram_tensor("stab", [128, NT * 32], F32, kind="ExternalInput").ap()
    eye_in = nc.dram_tensor("eye", [128, 128], BF16, kind="ExternalInput").ap()
    ones_in = nc.dram_tensor("onesc", [128, 4], F32, kind="ExternalInput").ap()
    y_out = nc.dram_tensor("y", [NT * 128, D], F32, kind="ExternalOutput").ap()
    y2_out = (nc.dram_tensor("y2", [NT * 128, D], F32, kind="ExternalOutput").ap()
              if dual_store else None)
    dbg = {}
    if debug:
        for nm, shp, dt in [("d_scalc", [128, 6], F32), ("d_w1q", [128, DH], BF16),
                            ("d_w2q", [128, D], BF16), ("d_kv", [128, D], F32),
                            ("d_ks", [128, D], F32), ("d_qf", [128, D], F32),
                            ("d_x1", [128, D], F32), ("d_xq1", [128, D], BF16),
                            ("d_hact", [128, DH], F32), ("d_att", [128, D], BF16),
                            ("d_u", [128, D], F32), ("d_hq", [128, DH], BF16),
                            ("d_u2", [128, D], F32), ("d_cols", [128, 40], F32),
                            ("d_ccin", [129, D], F32), ("d_ccout", [129, D], F32)]:
            dbg[nm] = nc.dram_tensor(nm, shp, dt, kind="ExternalOutput").ap()

    cc_in = nc.dram_tensor("cc_in", [129, D], F32)
    cc_out = nc.dram_tensor("cc_out", [129, D], F32)
    w2q_hbm = nc.dram_tensor("w2q_hbm", [DH, D], BF16)
    scal_hbm = nc.dram_tensor("scal_hbm", [1, 8], F32)

    es = contextlib.ExitStack()
    _s = es.enter_context

    w1q_s = _s(nc.sbuf_tensor("w1q_s", [128, KCH * DH], BF16))
    wq_s = _s(nc.sbuf_tensor("wq_s", [128, KCH * D], BF16))
    wout_s = _s(nc.sbuf_tensor("wout_s", [128, KCH * D], BF16))
    ct_s = _s(nc.sbuf_tensor("ct_s", [128, NT * 32], F32))
    st_s = _s(nc.sbuf_tensor("st_s", [128, NT * 32], F32))
    ksum_b = _s(nc.sbuf_tensor("ksum_b", [128, D], F32))
    kvaug = _s(nc.sbuf_tensor("kvaug", [128, D], BF16))
    eye_s = _s(nc.sbuf_tensor("eye_s", [128, 128], BF16))
    ones_s = _s(nc.sbuf_tensor("ones_s", [128, 4], F32))
    onesb_s = _s(nc.sbuf_tensor("onesb_s", [128, 1], BF16))
    cols = _s(nc.sbuf_tensor("cols", [128, 40], F32))
    stats = _s(nc.sbuf_tensor("stats", [128, 24], F32))
    scalc = _s(nc.sbuf_tensor("scalc", [128, 8], F32))

    scrA = _s(nc.sbuf_tensor("scrA", [128, D], F32))
    scrB = _s(nc.sbuf_tensor("scrB", [128, D], F32))
    scrC = _s(nc.sbuf_tensor("scrC", [128, D], F32))
    xb16 = [_s(nc.sbuf_tensor(f"xb16_{q}", [128, D], BF16)) for q in range(2)]
    xfm = [_s(nc.sbuf_tensor(f"xfm_{q}", [128, D], BF16)) for q in range(2)]

    p = Prog()
    cp = {}

    def dma(eng, out, in_, sem, waits=()):
        p.on(eng, lambda e, out=out, in_=in_: e.dma_start(out=out, in_=in_), waits=waits)
        return p.bump(eng, sem, 16)

    def rope_ops(e, src, dst, tile_i, tmp=None):
        tmp = tmp if tmp is not None else scrA
        cv = ct_s[:, 32 * tile_i:32 * (tile_i + 1)].unsqueeze(1).broadcast_to([128, H, 32])
        sv = st_s[:, 32 * tile_i:32 * (tile_i + 1)].unsqueeze(1).broadcast_to([128, H, 32])
        r3 = lambda ap: ap.rearrange("t (h two f) -> t h two f", h=H, two=2)
        xe, xo = r3(src[:])[:, :, 0, :], r3(src[:])[:, :, 1, :]
        re, ro = r3(dst[:])[:, :, 0, :], r3(dst[:])[:, :, 1, :]
        t1 = tmp[:, 0:512].rearrange("t (h f) -> t h f", h=H)
        t2 = tmp[:, 512:1024].rearrange("t (h f) -> t h f", h=H)
        e.tensor_tensor(t1, xe, cv, op=AluOpType.mult)
        e.tensor_tensor(t2, xo, sv, op=AluOpType.mult)
        e.tensor_tensor(re, t1, t2, op=AluOpType.subtract)
        e.tensor_tensor(t1, xe, sv, op=AluOpType.mult)
        e.tensor_tensor(t2, xo, cv, op=AluOpType.mult)
        return e.tensor_tensor(ro, t1, t2, op=AluOpType.add)

    with contextlib.ExitStack() as p0:
        wf = [p0.enter_context(nc.sbuf_tensor(f"wf_{i}", [128, DH], F32)) for i in range(2)]
        wqf = [p0.enter_context(nc.sbuf_tensor(f"wqf_{i}", [128, DH], F32)) for i in range(2)]
        wqb = [p0.enter_context(nc.sbuf_tensor(f"wqb_{i}", [128, D], BF16)) for i in range(2)]
        ps_tot = p0.enter_context(nc.psum_tensor("ps_tot", [2, 512], F32))

        dma("sync", ct_s[:], ct_in, "s_ld")
        dma("sync", st_s[:], st_in, "s_ld", waits=[("s_ld", p.now("s_ld"))])
        dma("sync", eye_s[:], eye_in, "s_ld", waits=[("s_ld", p.now("s_ld"))])
        c_consts = dma("sync", ones_s[:], ones_in, "s_ld",
                       waits=[("s_ld", p.now("s_ld"))])
        c_onesb = p.on("vector", lambda e: e.tensor_copy(onesb_s[:], ones_s[:, 0:1]),
                       waits=[("s_ld", c_consts)], inc=("s_v", 1), fence=True)
        for kk in range(KCH):
            dma("gpsimd", wq_s[:, kk * D:(kk + 1) * D],
                wqkvT[128 * kk:128 * (kk + 1), 0:D], "s_wc",
                waits=[("s_wc", p.now("s_wc"))])
            dma("gpsimd", wout_s[:, kk * D:(kk + 1) * D],
                woutT[128 * kk:128 * (kk + 1), :], "s_wc",
                waits=[("s_wc", p.now("s_wc"))])
        c_wcast = p.now("s_wc")

        p.on("vector", lambda e: e.memset(cols[:, 0:2], 0.0), waits=[("s_ld", c_consts)])
        c_wdma = {}
        for idx, (nch, src, width) in enumerate(((KCH, w1T, DH), (32, w2T, D))):
            for r in range(nch):
                key = (idx, r)
                w = [("s_v", c_wdma.get((idx, r - 2)))] if r >= 2 or idx > 0 else ()
                if idx == 1 and r < 2:
                    w = [("s_v", c_wdma.get((0, KCH - 2 + r)))]
                c = dma("sync", wf[r % 2][:, 0:width], src[128 * r:128 * (r + 1), :],
                        "s_l0" if r % 2 == 0 else "s_l1", waits=w)

                lsem = "s_l0" if r % 2 == 0 else "s_l1"
                if r % 2 == 1:
                    c_ab = p.on("scalar", lambda e, r=r, width=width: e.activation(
                        wqf[1][:, 0:width], wf[r % 2][:, 0:width], AF.Abs,
                        accum_out=cols[:, 3:4]),
                        waits=[(lsem, c), ("s_v", c_wdma.get((idx, r - 2)) or
                               (c_wdma.get((0, KCH - 1)) if idx == 1 else None) or 0)],
                        inc=("s_s", 1), fence=True)
                    c_wdma[key] = p.on("vector", lambda e, idx=idx: e.tensor_tensor(
                        cols[:, idx:idx + 1], cols[:, idx:idx + 1], cols[:, 3:4],
                        op=AluOpType.add),
                        waits=[("s_s", c_ab)], inc=("s_v", 1), fence=True)
                else:
                    def absacc(e, r=r, idx=idx, width=width):
                        e.tensor_reduce(cols[:, 2:3], wf[r % 2][:, 0:width], axis=AX.X,
                                        op=AluOpType.add, apply_absolute_value=True)
                        e.drain()
                        return e.tensor_tensor(cols[:, idx:idx + 1], cols[:, idx:idx + 1],
                                               cols[:, 2:3], op=AluOpType.add)
                    c_wdma[key] = p.on("vector", absacc,
                                       waits=[(lsem, c)],
                                       inc=("s_v", 1), fence=True)
        c_sums = p.now("s_v")
        c_tot = p.on("tensor", lambda e: e.matmul(
            ps_tot[0:1, 0:2], ones_s[:, 0:1], cols[:, 0:2], start=True, stop=True),
            waits=[("s_v", c_sums), ("s_ld", c_consts)], inc=("s_t", 1))

        def mk_scal(e):
            e.tensor_scalar(scalc[0:1, 4:6], ps_tot[0:1, 0:2], 1.0 / (D * DH), EPS,
                            op0=AluOpType.mult, op1=AluOpType.max)
            e.drain()
            e.reciprocal(scalc[0:1, 0:1], scalc[0:1, 4:5])
            e.reciprocal(scalc[0:1, 2:3], scalc[0:1, 5:6])
            e.tensor_scalar(scalc[0:1, 1:2], scalc[0:1, 4:5], 1.0 / 127.0, None,
                            op0=AluOpType.mult)
            return e.tensor_scalar(scalc[0:1, 3:4], scalc[0:1, 5:6], 1.0 / 127.0, None,
                                   op0=AluOpType.mult)
        c_scal = p.on("vector", mk_scal, waits=[("s_t", c_tot)], inc=("s_v", 1),
                      fence=True)
        c = dma("gpsimd", scal_hbm.ap()[0:1, 0:6], scalc[0:1, 0:6], "s_sc",
                waits=[("s_v", c_scal)])
        c_scalb = dma("gpsimd", scalc[:, 0:6], scal_hbm.ap()[0:1, 0:6].broadcast_to([128, 6]),
                      "s_sc", waits=[("s_sc", c)])

        c_sq = {"s": {}, "v": {}, "g": {}}
        for idx, (nch, src, width) in enumerate(((KCH, w1T, DH), (32, w2T, D))):
            for r in range(nch):
                gr = r + (KCH if idx == 1 else 0)
                b = gr % 2
                w = ([("s_s", c_sq["s"].get(gr - 2, 0))] if gr >= 2
                     else [("s_v", c_sums)])
                lsem = "s_l0" if b == 0 else "s_l1"
                c = dma("sync", wf[b][:, 0:width], src[128 * r:128 * (r + 1), :],
                        lsem, waits=w)
                sc = scalc[:, 0:1] if idx == 0 else scalc[:, 2:3]
                c_sq["s"][gr] = p.on(
                    "scalar", lambda e, b=b, width=width, sc=sc: e.activation(
                        wqf[b][:, 0:width], wf[b][:, 0:width], AF.Copy, bias=MAGIC,
                        scale=sc),
                    waits=[(lsem, c), ("s_sc", c_scalb),
                           ("s_v", c_sq["v"].get(gr - 2, 0))],
                    inc=("s_s", 1))

                def clip1(e, b=b, width=width):
                    return e.tensor_scalar(wqf[b][:, 0:width], wqf[b][:, 0:width],
                                           MAGIC, 1.0,
                                           op0=AluOpType.subtract, op1=AluOpType.min)
                wsem = "s_w2w" if b == 0 else "s_w2x"
                p.on("vector", clip1, waits=[("s_s", c_sq["s"][gr]),
                                             (wsem, c_sq["g"].get(gr - 2, 0))])
                if idx == 0:
                    c_sq["v"][gr] = p.on("vector", lambda e, r=r, b=b: e.tensor_scalar(
                        w1q_s[:, DH * r:DH * (r + 1)], wqf[b][:], -1.0, None,
                        op0=AluOpType.max), inc=("s_v", 1))
                else:
                    c_sq["v"][gr] = p.on("vector", lambda e, b=b: e.tensor_scalar(
                        wqb[b][:], wqf[b][:, 0:D], -1.0, None, op0=AluOpType.max),
                        inc=("s_v", 1))
                    c_sq["g"][gr] = dma("gpsimd",
                                        w2q_hbm.ap()[128 * r:128 * (r + 1), :],
                                        wqb[b][:], wsem,
                                        waits=[("s_v", c_sq["v"][gr])])
        if debug:
            dma("gpsimd", dbg["d_scalc"], scalc[:, 0:6], "s_gd", waits=[("s_sc", c_scalb)])
            dma("gpsimd", dbg["d_w1q"], w1q_s[:, 0:DH], "s_gd",
                waits=[("s_v", p.now("s_v"))])
            dma("gpsimd", dbg["d_w2q"], w2q_hbm.ap()[0:128, :], "s_gd",
                waits=[("s_w2w", p.now("s_w2w"))])
        c_p0_v = p.now("s_v")
        c_p0_s = p.now("s_s")
        c_p0_w2w = p.now("s_w2w")
        c_p0_w2x = p.now("s_w2x")

    with contextlib.ExitStack() as p1:
        wkv_s = p1.enter_context(nc.sbuf_tensor("wkv_s", [128, KCH * 2 * D], BF16))
        kf16 = [p1.enter_context(nc.sbuf_tensor(f"kf16_{q}", [128, D], BF16)) for q in range(2)]
        vb16 = [p1.enter_context(nc.sbuf_tensor(f"vb16_{q}", [128, D], BF16)) for q in range(2)]
        kvpart = p1.enter_context(nc.sbuf_tensor("kvpart", [128, D], F32))
        scrD = p1.enter_context(nc.sbuf_tensor("scrD1", [128, D], F32))
        ksrow = p1.enter_context(nc.sbuf_tensor("ksrow", [128, 512], F32))
        ps_kvacc = p1.enter_context(nc.psum_tensor("ps_kvacc", [128, D], F32))
        ps_k = p1.enter_context(nc.psum_tensor("ps_k", [128, D], F32))
        ps_v = p1.enter_context(nc.psum_tensor("ps_v", [128, D], F32))
        ps_xT = p1.enter_context(nc.psum_tensor("ps_xT", [128, D], BF16))
        ps_ks = p1.enter_context(nc.psum_tensor("ps_ks", [128, 512], F32))
        p.on("vector", lambda e: e.memset(kvpart[:], 0.0),
             waits=[("s_s", c_p0_s), ("s_w2w", c_p0_w2w), ("s_w2x", c_p0_w2x)])
        p.on("vector", lambda e: e.memset(ksrow[:], 0.0))

        for kk in range(KCH):
            c_wkv = dma("gpsimd", wkv_s[:, 2 * D * kk:2 * D * (kk + 1)],
                        wqkvT[128 * kk:128 * (kk + 1), D:3 * D], "s_wkv",
                        waits=([("s_w2w", c_p0_w2w), ("s_w2x", c_p0_w2x),
                                ("s_s", c_p0_s),
                                ("s_v", c_p0_v)] if kk == 0
                               else [("s_wkv", p.now("s_wkv"))]))

        p1B = p1.enter_context(nc.sbuf_tensor("p1B", [128, D], F32))
        p1C = p1.enter_context(nc.sbuf_tensor("p1C", [128, D], F32))
        p1D = p1.enter_context(nc.sbuf_tensor("p1D", [128, D], F32))
        rot = [scrB, p1B]
        me = [scrC, p1C]
        rl = [scrD, p1D]

        def p1_load(i):
            q = i % 2
            w = ([("s_t", cp.get(("p1_xT", i - 2)))] if i >= 2
                 else [("s_v", c_p0_v), ("s_s", c_p0_s)])
            xsem = "s_g0" if q == 0 else "s_g1"
            cp[("p1_xld", i)] = dma("gpsimd", xb16[q][:],
                                    x_in[128 * i:128 * (i + 1), :], xsem, waits=w)
            cp[("p1_xsem", i)] = xsem

        def p1_frontT(i):
            q = i % 2
            w = [(cp[("p1_xsem", i)], cp[("p1_xld", i)]), ("s_ld", c_consts),
                 ("s_s", cp.get(("p1_xfm_ev", i - 1)))]
            for kk in range(KCH):
                p.on("tensor", lambda e, kk=kk, q=q: e.transpose(
                    ps_xT[:, 128 * kk:128 * (kk + 1)], xb16[q][:, 128 * kk:128 * (kk + 1)],
                    eye_s[:]), waits=w if kk == 0 else ())
            cp[("p1_xT", i)] = p.bump("tensor", "s_t", 1)
            w = [("s_t", cp[("p1_xT", i)]), ("s_t", cp.get(("p1_vmm", i - 2)))]
            cp[("p1_xfm_ev", i)] = p.on(
                "scalar", lambda e, q=q: e.activation(xfm[q][:], ps_xT[:], AF.Copy),
                waits=w, inc=("s_s", 1))

        def p1_kmm(i):
            q = i % 2
            w = [("s_s", cp[("p1_xfm_ev", i)]), ("s_wkv", c_wkv),
                 ("s_v", cp.get(("p1_rope", i - 1)))]
            for ns in range(2):
                for kk in range(KCH):
                    p.on("tensor", lambda e, ns=ns, kk=kk, q=q: e.matmul(
                        ps_k[:, 512 * ns:512 * (ns + 1)], xfm[q][:, 128 * kk:128 * (kk + 1)],
                        wkv_s[:, 2 * D * kk + 512 * ns: 2 * D * kk + 512 * (ns + 1)],
                        start=(kk == 0), stop=(kk == KCH - 1), skip_group_check=True),
                       waits=w if (ns == 0 and kk == 0) else ())
            cp[("p1_kmm", i)] = p.bump("tensor", "s_t", 1)

        def p1_vmm(i):
            q = i % 2
            w = [("s_s", cp.get(("p1_vev", i - 1)))]
            for ns in range(2):
                for kk in range(KCH):
                    p.on("tensor", lambda e, ns=ns, kk=kk, q=q: e.matmul(
                        ps_v[:, 512 * ns:512 * (ns + 1)], xfm[q][:, 128 * kk:128 * (kk + 1)],
                        wkv_s[:, 2 * D * kk + D + 512 * ns: 2 * D * kk + D + 512 * (ns + 1)],
                        start=(kk == 0), stop=(kk == KCH - 1), skip_group_check=True),
                       waits=w if (ns == 0 and kk == 0) else ())
            cp[("p1_vmm", i)] = p.bump("tensor", "s_t", 1)

        def p1_rope(i):
            q = i % 2
            cp[("p1_rope", i)] = p.on(
                "vector", lambda e, i=i, q=q: rope_ops(e, ps_k, rot[q], i, tmp=scrA),
                waits=[("s_t", cp[("p1_kmm", i)])], inc=("s_v", 1))
            cp[("p1_min", i)] = p.on("vector", lambda e, q=q: e.tensor_scalar(
                me[q][:], rot[q][:], 0.0, None, op0=AluOpType.min), inc=("s_v", 1))

        def p1_feat(i):
            q = i % 2
            c_exp = p.on("scalar", lambda e, q=q: e.activation(me[q][:], me[q][:], AF.Exp),
                         waits=[("s_v", cp[("p1_min", i)])], inc=("s_s", 1))
            c_rel = p.on("scalar", lambda e, q=q: e.activation(rl[q][:], rot[q][:], AF.Relu),
                         inc=("s_s", 1))
            cp[("p1_kf", i)] = p.on("vector", lambda e, q=q: e.tensor_tensor(
                kf16[q][:], me[q][:], rl[q][:], op=AluOpType.add),
                waits=[("s_s", c_rel), ("s_t", cp.get(("p1_kv", i - 2)))], inc=("s_v", 1))
            cp[("p1_vev", i)] = p.on("scalar", lambda e, q=q: e.activation(
                vb16[q][:], ps_v[:], AF.Copy),
                waits=[("s_t", cp[("p1_vmm", i)]), ("s_t", cp.get(("p1_kv", i - 2)))],
                inc=("s_s", 1))

        def p1_acc(i):
            q = i % 2
            w = [("s_v", cp[("p1_kf", i)]), ("s_s", cp[("p1_vev", i)]),
                 ("s_v", c_onesb), ("s_v", cp.get(("p1_kvadd", i - 1)))]
            for c8 in range(KCH):
                p.on("tensor", lambda e, c8=c8, q=q: e.matmul(
                    ps_kvacc[:, 128 * c8:128 * (c8 + 1)], kf16[q][:, 128 * c8:128 * (c8 + 1)],
                    vb16[q][:, 128 * c8:128 * (c8 + 1)],
                    start=True, stop=True, skip_group_check=True),
                   waits=w if c8 == 0 else ())
            for ns, base in ((0, 0), (1, 32)):
                p.on("tensor", lambda e, ns=ns, base=base, q=q: e.matmul(
                    ps_ks[base:base + 1, 0:512], onesb_s[:],
                    kf16[q][:, 512 * ns:512 * (ns + 1)],
                    start=True, stop=True, skip_group_check=True,
                    tile_position=(0, base)))
            cp[("p1_kv", i)] = p.bump("tensor", "s_t", 1)

            def kvadd(e):
                e.tensor_tensor(kvpart[:], kvpart[:], ps_kvacc[:], op=AluOpType.add)
                e.tensor_tensor(ksrow[0:1, :], ksrow[0:1, :], ps_ks[0:1, :],
                                op=AluOpType.add)
                e.tensor_tensor(ksrow[32:33, :], ksrow[32:33, :], ps_ks[32:33, :],
                                op=AluOpType.add)
                return e.drain()
            cp[("p1_kvadd", i)] = p.on("vector", kvadd,
                                       waits=[("s_t", cp[("p1_kv", i)])],
                                       inc=("s_v", 1))

        p1_load(0)
        p1_frontT(0)
        p1_kmm(0)
        p1_vmm(0)
        for i in range(NT):
            p1_rope(i)
            if i >= 1:
                p1_acc(i - 1)
            if i + 1 < NT:
                p1_load(i + 1)
                p1_frontT(i + 1)
                p1_kmm(i + 1)
            p1_feat(i)
            if i + 1 < NT:
                p1_vmm(i + 1)
        p1_acc(NT - 1)
        c_p1_t = p.now("s_t")
        ck1 = cp[("p1_kvadd", NT - 1)]
        dma("gpsimd", cc_in.ap()[0:128, :], kvpart[:], "s_ci", waits=[("s_v", ck1)])
        dma("gpsimd", cc_in.ap()[128:129, 0:512], ksrow[0:1, :], "s_ci",
            waits=[("s_ci", p.now("s_ci"))])
        c_ccin = dma("gpsimd", cc_in.ap()[128:129, 512:1024], ksrow[32:33, :], "s_ci",
                     waits=[("s_ci", p.now("s_ci"))])
        c_cc = p.on("gpsimd", lambda e: e.collective_compute(
            "AllReduce", mybir.AluOpType.add,
            replica_groups=[[0, 1, 2, 3], [4, 5, 6, 7]],
            ins=[cc_in.ap().opt()], outs=[cc_out.ap().opt()]),
            waits=[("s_ci", c_ccin)], inc=("s_cc", 1))
        c_kvrd = dma("gpsimd", kvpart[:], cc_out.ap()[0:128, :], "s_kvr",
                     waits=[("s_cc", c_cc)])
        c_ksb = dma("gpsimd", ksum_b[:], cc_out.ap()[128:129, :].broadcast_to([128, D]),
                    "s_ksb")
        if debug:
            dma("gpsimd", dbg["d_ccin"], cc_in.ap(), "s_gd", waits=[("s_ksb", c_ksb)])
            dma("gpsimd", dbg["d_ccout"], cc_out.ap(), "s_gd")
            dma("gpsimd", dbg["d_kv"], kvpart[:], "s_gd")
            dma("gpsimd", dbg["d_ks"], ksum_b[:], "s_gd")
        p.on("vector", lambda e: e.memset(kvaug[:], 0.0), waits=[("s_kvr", c_kvrd)])
        for c8 in range(KCH):
            def cpk(e, c8=c8):
                e.tensor_copy(kvaug[0:64, 128 * c8:128 * c8 + 64],
                              kvpart[0:64, 128 * c8:128 * c8 + 64])
                return e.tensor_copy(kvaug[64:128, 128 * c8 + 64:128 * (c8 + 1)],
                                     kvpart[64:128, 128 * c8 + 64:128 * (c8 + 1)])
            p.on("vector", cpk)
        c_kvaug = p.bump("vector", "s_v", 1, fence=True)

    with contextlib.ExitStack() as p2:
        xb32 = [p2.enter_context(nc.sbuf_tensor(f"xb32_{q}", [128, D], F32)) for q in range(2)]
        x1b = [p2.enter_context(nc.sbuf_tensor(f"x1b_{q}", [128, D], F32)) for q in range(2)]
        h_act = p2.enter_context(nc.sbuf_tensor("h_act", [128, DH], F32))
        hqfm = [p2.enter_context(nc.sbuf_tensor(f"hqfm_{q}", [128, DH], BF16)) for q in range(2)]
        w2b = [p2.enter_context(nc.sbuf_tensor(f"w2b_{b}", [128, D], BF16)) for b in range(4)]
        yA = p2.enter_context(nc.sbuf_tensor("yA", [128, D], F32))
        yB = p2.enter_context(nc.sbuf_tensor("yB", [128, D], F32))
        bs1 = p2.enter_context(nc.sbuf_tensor("bs1", [128, D], BF16))
        bs2 = p2.enter_context(nc.sbuf_tensor("bs2", [128, D], BF16))
        bs3 = p2.enter_context(nc.sbuf_tensor("bs3", [128, D], BF16))
        bs4 = p2.enter_context(nc.sbuf_tensor("bs4", [128, D], BF16))
        bs5 = p2.enter_context(nc.sbuf_tensor("bs5", [128, D], BF16))
        bs6 = p2.enter_context(nc.sbuf_tensor("bs6", [128, D], BF16))
        ps_T = p2.enter_context(nc.psum_tensor("ps_T", [128, 2048], BF16))
        ps_q = p2.enter_context(nc.psum_tensor("ps_q", [128, D], F32))
        ps_at = p2.enter_context(nc.psum_tensor("ps_at", [128, D], F32))
        ps_f = p2.enter_context(nc.psum_tensor("ps_f", [128, D], F32))
        DUMP = h_act[:, 2048:3072]

        def tA_u(i):
            q = i % 2
            w = [("s_t", cp[("p2_womm", i)]), ("s_s", cp.get(("p2_s2", i - 1)))]
            c_u = p.on("vector", lambda e, q=q: e.tensor_tensor(
                scrC[:], ps_q[:], xb32[q][:], op=AluOpType.add),
                waits=w, inc=("s_v", 1))
            cp[("p2_u", i)] = c_u
            cp[("p2_psq_free", i)] = c_u
            if debug and i == DBG_TILE:
                cdbg = dma("gpsimd", dbg["d_u"], scrC[:], "s_gd", waits=[("s_v", c_u)])
                cdbg2 = dma("gpsimd", dbg["d_att"], bs5[:], "s_gd")
                c_u = p.on("vector", lambda e: e.memset(cols[:, 30:31], 0.0),
                           waits=[("s_gd", cdbg2)], inc=("s_v", 1))
                cp[("p2_u", i)] = c_u
                cp[("p2_psq_free", i)] = c_u

        def tA_front(i):
            q = i % 2
            w = [("s_v", cp.get(("p2_u", i - 2))), ("s_ksb", c_ksb)]
            xsem = "s_xa" if q == 0 else "s_xb"
            c_xld = dma("gpsimd", xb32[q][:], x_in[128 * i:128 * (i + 1), :], xsem,
                        waits=w)
            w = [(xsem, c_xld), ("s_t", cp.get(("p2_xT", i - 2)))]
            c_xb = p.on("vector", lambda e, q=q: e.tensor_copy(xb16[q][:], xb32[q][:]),
                        waits=w, inc=("s_v", 1))
            w = [("s_v", c_xb), ("s_s", cp.get(("p2_xfm_ev", i - 1))),
                 ("s_v", c_kvaug)]
            for kk in range(KCH):
                p.on("tensor", lambda e, kk=kk, q=q: e.transpose(
                    ps_T[:, 128 * kk:128 * (kk + 1)], xb16[q][:, 128 * kk:128 * (kk + 1)],
                    eye_s[:]), waits=w if kk == 0 else ())
            cp[("p2_xT", i)] = p.bump("tensor", "s_t", 1)
            w = [("s_t", cp[("p2_xT", i)]), ("s_t", cp.get(("p2_qmm", i - 2)))]
            cp[("p2_xfm_ev", i)] = p.on(
                "scalar", lambda e, q=q: e.activation(xfm[q][:], ps_T[:, 0:1024], AF.Copy),
                waits=w, inc=("s_s", 1))
            w = [("s_s", cp[("p2_xfm_ev", i)]), ("s_wc", c_wcast),
                 ("s_v", cp.get(("p2_psq_free", i - 1)))]
            for ns in range(2):
                for kk in range(KCH):
                    p.on("tensor", lambda e, ns=ns, kk=kk, q=q: e.matmul(
                        ps_q[:, 512 * ns:512 * (ns + 1)], xfm[q][:, 128 * kk:128 * (kk + 1)],
                        wq_s[:, D * kk + 512 * ns: D * kk + 512 * (ns + 1)],
                        start=(kk == 0), stop=(kk == KCH - 1), skip_group_check=True),
                       waits=w if (ns == 0 and kk == 0) else ())
            cp[("p2_qmm", i)] = p.bump("tensor", "s_t", 1)

        def tA_mid1(i):
            w = [("s_t", cp[("p2_qmm", i)])]
            c_rope = p.on("vector", lambda e, i=i: rope_ops(e, ps_q, scrB, i, tmp=yA),
                          waits=w, inc=("s_v", 1))
            c_min = p.on("vector", lambda e: e.tensor_scalar(
                yB[:], scrB[:], 0.125, 0.0, op0=AluOpType.mult, op1=AluOpType.min),
                inc=("s_v", 1))
            c_exp = p.on("scalar", lambda e: e.activation(yB[:], yB[:], AF.Exp),
                         waits=[("s_v", c_min)], inc=("s_s", 1))
            c_rel = p.on("scalar", lambda e: e.activation(
                yA[:], scrB[:], AF.Relu, scale=0.125), inc=("s_s", 1))
            c_qf = p.on("vector", lambda e: e.tensor_tensor(
                scrB[:], yB[:], yA[:], op=AluOpType.add),
                waits=[("s_s", c_rel)], inc=("s_v", 1))
            cp[("p2_qf", i)] = c_qf
            if debug and i == DBG_TILE:
                cdbg = dma("gpsimd", dbg["d_qf"], scrB[:], "s_gd", waits=[("s_v", c_qf)])
                c_qf = p.on("vector", lambda e: e.memset(cols[:, 30:31], 0.0),
                            waits=[("s_gd", cdbg)], inc=("s_v", 1))
                cp[("p2_qf", i)] = c_qf

            def denom(e):
                e.tensor_tensor(yA[:], scrB[:], ksum_b[:], op=AluOpType.mult)
                e.tensor_reduce(cols[:, 0:16],
                                yA[:].rearrange("t (h f) -> t h f", h=H), axis=AX.X,
                                op=AluOpType.add)
                e.drain()
                e.tensor_scalar(cols[:, 0:16], cols[:, 0:16], 1e-6, None,
                                op0=AluOpType.max)
                e.drain()
                e.reciprocal(cols[:, 16:32], cols[:, 0:16])
                return e.drain()
            p.on("vector", denom, waits=[("s_ksb", c_ksb)])
            w = [("s_t", cp.get(("p2_qfsT", i - 1)))]
            for h in range(H):
                p.on("vector", lambda e, h=h: e.tensor_scalar(
                    bs1[:, 64 * h:64 * (h + 1)], scrB[:, 64 * h:64 * (h + 1)],
                    cols[:, 16 + h:17 + h], None, op0=AluOpType.mult),
                   waits=w if h == 0 else ())
            cp[("p2_qfs", i)] = p.bump("vector", "s_v", 1, fence=True)

        def tA_midT(i):
            w = [("s_v", cp[("p2_qfs", i)]), ("s_s", cp.get("psTqB_ev"))]
            for kk in range(KCH):
                p.on("tensor", lambda e, kk=kk: e.transpose(
                    ps_T[:, 1024 + 128 * kk:1024 + 128 * (kk + 1)],
                    bs1[:, 128 * kk:128 * (kk + 1)], eye_s[:]),
                   waits=w if kk == 0 else ())
            cp[("p2_qfsT", i)] = p.bump("tensor", "s_t", 1)
            w = [("s_t", cp[("p2_qfsT", i)]), ("s_t", cp.get(("p2_atmm", i - 1)))]
            c_qff = p.on("scalar", lambda e: e.activation(bs4[:], ps_T[:, 1024:2048], AF.Copy),
                         waits=w, inc=("s_s", 1))
            cp["psTqB_ev"] = c_qff
            w = [("s_s", c_qff), ("s_v", c_kvaug),
                 ("s_v", cp.get("psat_free")), ("s_s", cp.get(("p2_ate", i - 1)))]
            for c8 in range(KCH):
                p.on("tensor", lambda e, c8=c8: e.matmul(
                    ps_at[:, 128 * c8:128 * (c8 + 1)], kvaug[:, 128 * c8:128 * (c8 + 1)],
                    bs4[:, 128 * c8:128 * (c8 + 1)], start=True, stop=True,
                    skip_group_check=True), waits=w if c8 == 0 else ())
            cp[("p2_atmm", i)] = p.bump("tensor", "s_t", 1)
            w = [("s_t", cp[("p2_atmm", i)]), ("s_t", cp.get(("p2_womm", i - 1)))]
            c_ate = p.on("scalar", lambda e: e.activation(bs5[:], ps_at[:], AF.Copy),
                         waits=w, inc=("s_s", 1))
            cp[("p2_ate", i)] = c_ate
            w = [("s_s", c_ate), ("s_v", cp[("p2_qf", i)])]
            for ns in range(2):
                for kk in range(KCH):
                    p.on("tensor", lambda e, ns=ns, kk=kk: e.matmul(
                        ps_q[:, 512 * ns:512 * (ns + 1)], bs5[:, 128 * kk:128 * (kk + 1)],
                        wout_s[:, D * kk + 512 * ns: D * kk + 512 * (ns + 1)],
                        start=(kk == 0), stop=(kk == KCH - 1), skip_group_check=True),
                       waits=w if (ns == 0 and kk == 0) else ())
            cp[("p2_womm", i)] = p.bump("tensor", "s_t", 1)

        def tA_backA(i):
            q = i % 2
            s = i % 2
            c_u = cp[("p2_u", i)]
            p.on("scalar", lambda e: e.activation(
                DUMP, scrC[:], AF.Identity, accum_out=stats[:, 0:1]),
                waits=[("s_v", c_u)], inc=("s_s", 1))
            c_s2 = p.on("scalar", lambda e: e.activation(
                DUMP, scrC[:], AF.Square, accum_out=stats[:, 1:2]),
                inc=("s_s", 1), fence=True)
            cp[("p2_s2", i)] = c_s2

            def ln1cols(e):
                e.tensor_scalar(stats[:, 2:3], stats[:, 0:1], 1.0 / D, None,
                                op0=AluOpType.mult)
                e.tensor_tensor(stats[:, 3:4], stats[:, 2:3], stats[:, 2:3],
                                op=AluOpType.mult)
                e.tensor_scalar(stats[:, 4:5], stats[:, 1:2], 1.0 / D, None,
                                op0=AluOpType.mult)
                e.tensor_tensor(stats[:, 4:5], stats[:, 4:5], stats[:, 3:4],
                                op=AluOpType.subtract)
                e.tensor_scalar(stats[:, 3:4], stats[:, 4:5], EPS, None,
                                op0=AluOpType.add)
                e.tensor_tensor(stats[:, 5:6], stats[:, 3:4], ones_s[:, 1:2],
                                op=AluOpType.pow)
                return e.tensor_tensor(stats[:, 18:19], stats[:, 3:4], ones_s[:, 1:2],
                                       op=AluOpType.pow)
            c_rs = p.on("gpsimd", ln1cols, waits=[("s_s", c_s2)], inc=("s_gc", 1))
            c_x1 = p.on("vector", lambda e, s=s: e.tensor_scalar(
                x1b[s][:], scrC[:], stats[:, 2:3], stats[:, 5:6],
                op0=AluOpType.subtract, op1=AluOpType.mult),
                waits=[("s_gc", c_rs)], inc=("s_v", 1))

            c_am1 = p.on("vector", lambda e, s=s: e.tensor_reduce(
                cols[:, 32:33], x1b[s][:], axis=AX.X, op=AluOpType.max,
                apply_absolute_value=True),
                waits=[("s_v", c_x1)], inc=("s_v", 1), fence=True)

            def q1cols(e, s=s):
                e.tensor_scalar(cols[:, 32:33], cols[:, 32:33], EPS, None,
                                op0=AluOpType.max)
                e.tensor_tensor(cols[:, 33:34], cols[:, 32:33], ones_s[:, 3:4],
                                op=AluOpType.pow)
                e.tensor_scalar(cols[:, 33:34], cols[:, 33:34], 127.0, None,
                                op0=AluOpType.mult)
                e.tensor_tensor(cols[:, 34 + s:35 + s], cols[:, 32:33], scalc[:, 1:2],
                                op=AluOpType.mult)
                return e.tensor_tensor(cols[:, 30:31], cols[:, 32:33], scalc[:, 1:2],
                                       op=AluOpType.mult)
            c_q1g = p.on("gpsimd", q1cols, waits=[("s_v", c_am1)], inc=("s_gc", 1))

            def q1(e, s=s):
                e.tensor_scalar(scrA[:], x1b[s][:], cols[:, 33:34], MAGIC,
                                op0=AluOpType.mult, op1=AluOpType.add)
                e.tensor_scalar(scrA[:], scrA[:], MAGIC, 127.0,
                                op0=AluOpType.subtract, op1=AluOpType.min)
                return e.tensor_scalar(bs2[:], scrA[:], -128.0, None, op0=AluOpType.max)
            if debug and i == DBG_TILE:
                dma("gpsimd", dbg["d_x1"], x1b[s][:], "s_gd", waits=[("s_v", c_x1)])
            c_q1 = p.on("vector", q1,
                        waits=[("s_gc", c_q1g), ("s_t", cp.get(("p2_xq1T", i - 1)))],
                        inc=("s_v", 1))
            cp[("p2_q1", i)] = c_q1
            w = [("s_v", c_q1), ("s_s", cp.get("psTqB_ev"))]
            for kk in range(KCH):
                p.on("tensor", lambda e, kk=kk: e.transpose(
                    ps_T[:, 1024 + 128 * kk:1024 + 128 * (kk + 1)],
                    bs2[:, 128 * kk:128 * (kk + 1)], eye_s[:]),
                   waits=w if kk == 0 else ())
            cp[("p2_xq1T", i)] = p.bump("tensor", "s_t", 1)
            w = [("s_t", cp[("p2_xq1T", i)]), ("s_t", cp.get(("p2_f1mm", i - 1)))]
            c_xqf = p.on("scalar", lambda e: e.activation(bs6[:], ps_T[:, 1024:2048], AF.Copy),
                         waits=w, inc=("s_s", 1))
            cp["psTqB_ev"] = c_xqf

            gcp = []
            for nsl in range(8):
                half = ps_f[:, 512 * (nsl % 2):512 * (nsl % 2 + 1)]
                w = [("s_s", c_xqf), ("s_v", c_p0_v)]
                if nsl >= 2:
                    w.append(("s_s", gcp[nsl - 2]))
                else:
                    w.append(("s_v", cp.get("psf_free")))
                for kk in range(KCH):
                    p.on("tensor", lambda e, half=half, nsl=nsl, kk=kk: e.matmul(
                        half, bs6[:, 128 * kk:128 * (kk + 1)],
                        w1q_s[:, DH * kk + 512 * nsl: DH * kk + 512 * (nsl + 1)],
                        start=(kk == 0), stop=(kk == KCH - 1), skip_group_check=True),
                       waits=w if kk == 0 else ())
                c_mm = p.bump("tensor", "s_t", 1)
                if nsl == 7:
                    cp[("p2_f1mm", i)] = c_mm
                gw = [("s_t", c_mm), ("s_gc", c_rs)]
                if nsl < 2:
                    gw.append(("s_oa" if nsl == 0 else "s_ob",
                               cp.get(("p2_out", i // 2 - 1, nsl))))
                cg = p.on("scalar", lambda e, half=half, nsl=nsl, s=s: e.activation(
                    h_act[:, 512 * nsl:512 * (nsl + 1)], half, AF.Gelu,
                    scale=cols[:, 34 + s:35 + s]),
                    waits=gw, inc=("s_s", 1))
                gcp.append(cg)
            cp[("p2_gl", i)] = gcp[-1]


        def tA_backB(i):
            q = i % 2
            s = i % 2
            def q2(e, s=s):
                return e.tensor_reduce(cols[:, 36:37], h_act[:], axis=AX.X,
                                       op=AluOpType.max, apply_absolute_value=True)

            def q2cols(e, s=s):
                e.tensor_scalar(cols[:, 36:37], cols[:, 36:37], EPS, None,
                                op0=AluOpType.max)
                e.tensor_tensor(cols[:, 37:38], cols[:, 36:37], ones_s[:, 3:4],
                                op=AluOpType.pow)
                e.tensor_scalar(cols[:, 37:38], cols[:, 37:38], 127.0, None,
                                op0=AluOpType.mult)
                e.tensor_tensor(cols[:, 38 + s:39 + s], cols[:, 36:37],
                                scalc[:, 3:4], op=AluOpType.mult)
                return e.tensor_tensor(cols[:, 30:31], cols[:, 36:37],
                                       scalc[:, 3:4], op=AluOpType.mult)
            if debug and i == DBG_TILE:
                cdbg = dma("gpsimd", dbg["d_xq1"], bs2[:], "s_gd",
                           waits=[("s_v", cp[("p2_q1", i)])])
                cdbg2 = dma("gpsimd", dbg["d_hact"], h_act[:], "s_gd",
                            waits=[("s_s", cp[("p2_gl", i)])])
            c_q2 = p.on("vector", q2, waits=[("s_s", cp[("p2_gl", i)])], inc=("s_v", 1),
                        fence=True)
            c_q2 = p.on("gpsimd", q2cols, waits=[("s_v", c_q2)], inc=("s_gc", 1))
            if debug and i == DBG_TILE:
                c_q2 = p.on("vector", lambda e: e.memset(cols[:, 30:31], 0.0),
                            waits=[("s_gd", cdbg2)], inc=("s_v", 1))
            prev_T = cp.get(("p2_hqT", i - 1))
            for nsl in range(4):
                def qs(e, nsl=nsl):
                    e.tensor_scalar(scrA[:], h_act[:, 1024 * nsl:1024 * (nsl + 1)],
                                    cols[:, 37:38], MAGIC, op0=AluOpType.mult,
                                    op1=AluOpType.add)
                    e.tensor_scalar(scrA[:], scrA[:], MAGIC, 127.0,
                                    op0=AluOpType.subtract, op1=AluOpType.min)
                    return e.tensor_scalar(bs3[:], scrA[:], -128.0, None,
                                           op0=AluOpType.max)
                w = [("s_gc", c_q2), ("s_t", prev_T)]
                c_qs = p.on("vector", qs, waits=w, inc=("s_v", 1))
                w = [("s_v", c_qs), ("s_s", cp.get("psTqB_ev"))]
                for kk in range(KCH):
                    p.on("tensor", lambda e, kk=kk: e.transpose(
                        ps_T[:, 1024 + 128 * kk:1024 + 128 * (kk + 1)],
                        bs3[:, 128 * kk:128 * (kk + 1)], eye_s[:]),
                       waits=w if kk == 0 else ())
                prev_T = p.bump("tensor", "s_t", 1)
                cp["psTqB_ev"] = p.on(
                    "scalar", lambda e, s=s, nsl=nsl: e.activation(
                        hqfm[s][:, 1024 * nsl:1024 * (nsl + 1)], ps_T[:, 1024:2048],
                        AF.Copy), waits=[("s_t", prev_T)], inc=("s_s", 1))
            cp[("p2_hqT", i)] = prev_T
            cp[("p2_hq_ready", i)] = p.now("s_s")
            if debug and i == DBG_TILE:
                dma("gpsimd", dbg["d_hq"], hqfm[DBG_TILE % 2][:], "s_gd",
                    waits=[("s_s", cp[("p2_hq_ready", i)])])

        def pair_B(j):
            i0, i1 = 2 * j, 2 * j + 1
            for kk in range(32):
                b = kk % 4
                if kk >= 4:
                    w = [("s_t", cp.get(("p2_w2u", kk - 4, j)))]
                elif j >= 1:
                    w = [("s_t", cp.get(("p2_w2u", 28 + kk, j - 1)))]
                else:
                    w = [("s_ksb", c_ksb)]
                wsem = f"s_w2{b}"
                c_wl = dma("sync", w2b[b][:], w2q_hbm.ap()[128 * kk:128 * (kk + 1), :],
                           wsem, waits=w)
                w = [(wsem, c_wl), ("s_s", cp[("p2_hq_ready", i1)])]
                if kk == 0:
                    w.append(("s_s", cp.get(("p2_ate", i1 + 1))))
                first = True
                for s, tgt in ((0, ps_f), (1, ps_at)):
                    for nh in range(2):
                        p.on("tensor", lambda e, tgt=tgt, s=s, kk=kk, nh=nh: e.matmul(
                            tgt[:, 512 * nh:512 * (nh + 1)],
                            hqfm[s][:, 128 * kk:128 * (kk + 1)],
                            w2b[kk % 4][:, 512 * nh:512 * (nh + 1)],
                            start=(kk == 0), stop=(kk == 31), skip_group_check=True),
                           waits=w if first else ())
                        first = False
                cp[("p2_w2u", kk, j)] = p.bump("tensor", "s_t", 1)
            c_f2 = p.now("s_t")
            cp[("p2_f2mm", i0)] = c_f2
            cp[("p2_f2mm", i1)] = c_f2
            for s, i, tgt in ((0, i0, ps_f), (1, i1, ps_at)):
                b2 = 6 + 6 * s
                ofm = h_act[:, 1024 * s:1024 * (s + 1)]
                osem = "s_oa" if s == 0 else "s_ob"
                w = [("s_t", c_f2), (osem, cp.get(("p2_out", j - 1, s)))]

                def evac(e, s=s, tgt=tgt):
                    return e.scalar_tensor_tensor(scrC[:], tgt[:], cols[:, 38 + s:39 + s],
                                                  x1b[s][:], op0=AluOpType.mult,
                                                  op1=AluOpType.add)
                c_ev = p.on("vector", evac, waits=w, inc=("s_v", 1))
                if s == 0:
                    cp["psf_free"] = c_ev
                else:
                    cp["psat_free"] = c_ev
                if debug and j == DBG_TILE // 2 and s == DBG_TILE % 2:
                    cdbg = dma("gpsimd", dbg["d_u2"], scrC[:], "s_gd", waits=[("s_v", c_ev)])
                    c_ev = p.on("vector", lambda e: e.memset(cols[:, 30:31], 0.0),
                                waits=[("s_gd", cdbg)], inc=("s_v", 1))
                p.on("scalar", lambda e, b2=b2: e.activation(
                    DUMP, scrC[:], AF.Identity, accum_out=stats[:, b2:b2 + 1]),
                    waits=[("s_v", c_ev)], inc=("s_s", 1))
                c_es2 = p.on("scalar", lambda e, b2=b2: e.activation(
                    DUMP, scrC[:], AF.Square, accum_out=stats[:, b2 + 1:b2 + 2]),
                    inc=("s_s", 1), fence=True)
                if s == 1:
                    cp[("p2_s2", i)] = c_es2

                def ln2cols(e, b2=b2):
                    e.tensor_scalar(stats[:, b2 + 2:b2 + 3], stats[:, b2:b2 + 1],
                                    1.0 / D, None, op0=AluOpType.mult)
                    e.tensor_tensor(stats[:, b2 + 4:b2 + 5], stats[:, b2 + 2:b2 + 3],
                                    stats[:, b2 + 2:b2 + 3], op=AluOpType.mult)
                    e.tensor_scalar(stats[:, b2 + 3:b2 + 4], stats[:, b2 + 1:b2 + 2],
                                    1.0 / D, None, op0=AluOpType.mult)
                    e.tensor_tensor(stats[:, b2 + 3:b2 + 4], stats[:, b2 + 3:b2 + 4],
                                    stats[:, b2 + 4:b2 + 5], op=AluOpType.subtract)
                    e.tensor_scalar(stats[:, b2 + 3:b2 + 4],
                                    stats[:, b2 + 3:b2 + 4], EPS, None,
                                    op0=AluOpType.add)
                    e.tensor_tensor(stats[:, b2 + 5:b2 + 6], stats[:, b2 + 3:b2 + 4],
                                    ones_s[:, 1:2], op=AluOpType.pow)
                    return e.tensor_tensor(stats[:, b2 + 4:b2 + 5],
                                           stats[:, b2 + 3:b2 + 4],
                                           ones_s[:, 1:2], op=AluOpType.pow)
                c_rs2 = p.on("gpsimd", ln2cols, waits=[("s_s", c_es2)],
                             inc=("s_gc", 1))

                def fin(e, b2=b2, ofm=ofm):
                    return e.tensor_scalar(ofm, scrC[:], stats[:, b2 + 2:b2 + 3],
                                           stats[:, b2 + 5:b2 + 6],
                                           op0=AluOpType.subtract, op1=AluOpType.mult)
                c_fin = p.on("vector", fin,
                             waits=[("s_gc", c_rs2)], inc=("s_v", 1), fence=True)
                cp[("p2_out", j, s)] = dma(
                    "scalar", y_out[128 * i:128 * (i + 1), :], ofm, osem,
                    waits=[("s_v", c_fin)])
                if dual_store:
                    cp[("p2_out", j, s)] = dma(
                        "scalar", y2_out[128 * i:128 * (i + 1), :], ofm, osem,
                        waits=[(osem, p.now(osem))])

        tA_front(0)
        tA_mid1(0)
        tA_midT(0)
        for i in range(NT):
            tA_u(i)
            if i + 1 < NT:
                tA_front(i + 1)
            tA_backA(i)
            if i + 1 < NT:
                tA_mid1(i + 1)
                tA_midT(i + 1)
            tA_backB(i)
            if i % 2 == 1:
                pair_B(i // 2)

        p.on("scalar", lambda e: None,
             waits=[("s_oa", p.now("s_oa")), ("s_ob", p.now("s_ob"))])

    sem_names = set()
    for eng in p.ENGINES:
        for waits, fn, inc, fence in p.items[eng]:
            for sname, v in waits:
                sem_names.add(sname)
            if inc:
                sem_names.add(inc[0])
    with contextlib.ExitStack() as semstack:
        sems = {sn: semstack.enter_context(nc.semaphore(sn)) for sn in sorted(sem_names)}
        with nc.Block() as block:
            p.emit(block, sems)
    es.close()
    return nc


_ROPE_BASE = 10000.0


def _tables(n_tiles, pos0):
    t = np.arange(pos0, pos0 + n_tiles * 128, dtype=np.float32)
    inv = (1.0 / _ROPE_BASE ** (np.arange(0, HD, 2, dtype=np.float32) / HD)).astype(np.float32)
    fr = (t[:, None] * inv[None, :]).astype(np.float32)
    cos = np.cos(fr.astype(np.float64)).astype(np.float32)
    sin = np.sin(fr.astype(np.float64)).astype(np.float32)

    def tile(a):
        return np.ascontiguousarray(
            a.reshape(n_tiles, 128, 32).transpose(1, 0, 2).reshape(128, n_tiles * 32))
    return tile(cos), tile(sin)


def kernel(L0_attn_x, L0_attn_w_qkv, L0_attn_w_out, L0_ffn_w1, L0_ffn_w2):
    B, T, _ = L0_attn_x.shape
    tokens_per_core = B * T // NCORES
    n_tiles = tokens_per_core // 128
    cores_per_batch = NCORES // B

    x_flat = np.ascontiguousarray(np.asarray(L0_attn_x, dtype=np.float32).reshape(B * T, D))
    wqkvT = np.ascontiguousarray(np.asarray(L0_attn_w_qkv, dtype=np.float32).T)
    woutT = np.ascontiguousarray(np.asarray(L0_attn_w_out, dtype=np.float32).T)
    w1T = np.ascontiguousarray(np.asarray(L0_ffn_w1, dtype=np.float32).T)
    w2T = np.ascontiguousarray(np.asarray(L0_ffn_w2, dtype=np.float32).T)
    eye = np.eye(128, dtype=ml_dtypes.bfloat16)
    onesc = np.ascontiguousarray(
        np.stack([np.ones(128, np.float32), np.full(128, -0.5, np.float32),
                  np.full(128, 127.0, np.float32), np.full(128, -1.0, np.float32)],
                 axis=1))

    nc = build(n_tiles)
    in_maps = []
    for c in range(NCORES):
        pos0 = (c % cores_per_batch) * tokens_per_core
        ct, st = _tables(n_tiles, pos0)
        in_maps.append({
            "x": x_flat[c * tokens_per_core:(c + 1) * tokens_per_core],
            "wqkvT": wqkvT, "woutT": woutT, "w1T": w1T, "w2T": w2T,
            "ctab": ct, "stab": st, "eye": eye, "onesc": onesc,
        })
    res = run_bass_kernel_spmd(nc, in_maps, core_ids=list(range(NCORES)))
    out = np.concatenate([np.asarray(r["y"]) for r in res.results], axis=0)
    return out.reshape(B, T, D).astype(np.float32)


# version 20
# speedup vs baseline: 1.1960x; 1.0482x over previous
import contextlib
import numpy as np
import ml_dtypes

import concourse.bass as bass
import concourse.mybir as mybir
from concourse.alu_op_type import AluOpType
from concourse.bass_utils import run_bass_kernel_spmd

F32 = mybir.dt.float32
BF16 = mybir.dt.bfloat16
AF = mybir.ActivationFunctionType
AX = mybir.AxisListType

D = 1024
H = 16
HD = 64
DH = 4096
KCH = 8
MAGIC = 3 * 2.0 ** 22
EPS = 1e-5
NCORES = 8


class Prog:

    ENGINES = ("sync", "tensor", "vector", "scalar", "gpsimd")

    def __init__(self):
        self.items = {e: [] for e in self.ENGINES}
        self.cnt = {}
        self.hw = {e: {} for e in self.ENGINES}

    def now(self, sem):
        return self.cnt.get(sem, 0)

    def on(self, eng, fn, waits=(), inc=None, fence=False):
        w = []
        for sem, val in waits:
            if val is None or val <= 0:
                continue
            if val > self.hw[eng].get(sem, 0):
                self.hw[eng][sem] = val
                w.append((sem, val))
        self.items[eng].append([w, fn, None, fence])
        if inc is not None:
            return self.bump(eng, *inc)
        return None

    def bump(self, eng, sem, k, fence=False):
        assert self.items[eng][-1][2] is None
        self.items[eng][-1][2] = (sem, k)
        if fence:
            self.items[eng][-1][3] = True
        self.cnt[sem] = self.cnt.get(sem, 0) + k
        return self.cnt[sem]

    def emit(self, block, sems):
        for name in self.ENGINES:
            eng_items = self.items[name]

            def make(eng_items=eng_items):
                def body(e):
                    for waits, fn, inc, fence in eng_items:
                        for sem, val in waits:
                            e.wait_ge(sems[sem], val)
                        ins = fn(e)
                        if fence:
                            ins = e.drain()
                        if inc is not None:
                            assert ins is not None, "inc on op returning None"
                            ins.then_inc(sems[inc[0]], inc[1])
                return body

            getattr(block, name)(make())


DBG_TILE = 8


def build(n_tiles, debug=False, dual_store=False):
    assert n_tiles % 2 == 0
    NT = n_tiles
    nc = bass.Bass("TRN2", target_bir_lowering=False, detect_race_conditions=False)

    x_in = nc.dram_tensor("x", [NT * 128, D], F32, kind="ExternalInput").ap()
    wqkvT = nc.dram_tensor("wqkvT", [D, 3 * D], F32, kind="ExternalInput").ap()
    woutT = nc.dram_tensor("woutT", [D, D], F32, kind="ExternalInput").ap()
    w1T = nc.dram_tensor("w1T", [D, DH], F32, kind="ExternalInput").ap()
    w2T = nc.dram_tensor("w2T", [DH, D], F32, kind="ExternalInput").ap()
    ct_in = nc.dram_tensor("ctab", [128, NT * 32], F32, kind="ExternalInput").ap()
    st_in = nc.dram_tensor("stab", [128, NT * 32], F32, kind="ExternalInput").ap()
    eye_in = nc.dram_tensor("eye", [128, 128], BF16, kind="ExternalInput").ap()
    ones_in = nc.dram_tensor("onesc", [128, 4], F32, kind="ExternalInput").ap()
    y_out = nc.dram_tensor("y", [NT * 128, D], F32, kind="ExternalOutput").ap()
    y2_out = (nc.dram_tensor("y2", [NT * 128, D], F32, kind="ExternalOutput").ap()
              if dual_store else None)
    dbg = {}
    if debug:
        for nm, shp, dt in [("d_scalc", [128, 6], F32), ("d_w1q", [128, DH], BF16),
                            ("d_w2q", [128, D], BF16), ("d_kv", [128, D], F32),
                            ("d_ks", [128, D], F32), ("d_qf", [128, D], F32),
                            ("d_x1", [128, D], F32), ("d_xq1", [128, D], BF16),
                            ("d_hact", [128, DH], F32), ("d_att", [128, D], BF16),
                            ("d_u", [128, D], F32), ("d_hq", [128, DH], BF16),
                            ("d_u2", [128, D], F32), ("d_cols", [128, 40], F32),
                            ("d_ccin", [129, D], F32), ("d_ccout", [129, D], F32)]:
            dbg[nm] = nc.dram_tensor(nm, shp, dt, kind="ExternalOutput").ap()

    cc_in = nc.dram_tensor("cc_in", [129, D], F32)
    cc_out = nc.dram_tensor("cc_out", [129, D], F32)
    w2q_hbm = nc.dram_tensor("w2q_hbm", [DH, D], BF16)
    scal_hbm = nc.dram_tensor("scal_hbm", [1, 8], F32)

    es = contextlib.ExitStack()
    _s = es.enter_context

    w1q_s = _s(nc.sbuf_tensor("w1q_s", [128, KCH * DH], BF16))
    wq_s = _s(nc.sbuf_tensor("wq_s", [128, KCH * D], BF16))
    wout_s = _s(nc.sbuf_tensor("wout_s", [128, KCH * D], BF16))
    ct_s = _s(nc.sbuf_tensor("ct_s", [128, NT * 32], F32))
    st_s = _s(nc.sbuf_tensor("st_s", [128, NT * 32], F32))
    ksum_b = _s(nc.sbuf_tensor("ksum_b", [128, D], F32))
    kvaug = _s(nc.sbuf_tensor("kvaug", [128, D], BF16))
    eye_s = _s(nc.sbuf_tensor("eye_s", [128, 128], BF16))
    ones_s = _s(nc.sbuf_tensor("ones_s", [128, 4], F32))
    onesb_s = _s(nc.sbuf_tensor("onesb_s", [128, 1], BF16))
    cols = _s(nc.sbuf_tensor("cols", [128, 40], F32))
    stats = _s(nc.sbuf_tensor("stats", [128, 24], F32))
    scalc = _s(nc.sbuf_tensor("scalc", [128, 8], F32))

    scrA = _s(nc.sbuf_tensor("scrA", [128, D], F32))
    scrC = _s(nc.sbuf_tensor("scrC", [128, D], F32))
    scrB = scrC
    xb16 = [_s(nc.sbuf_tensor(f"xb16_{q}", [128, D], BF16)) for q in range(2)]
    xfm = [_s(nc.sbuf_tensor(f"xfm_{q}", [128, D], BF16)) for q in range(2)]

    p = Prog()
    cp = {}

    def dma(eng, out, in_, sem, waits=()):
        p.on(eng, lambda e, out=out, in_=in_: e.dma_start(out=out, in_=in_), waits=waits)
        return p.bump(eng, sem, 16)

    def rope_ops(e, src, dst, tile_i, tmp=None):
        tmp = tmp if tmp is not None else scrA
        cv = ct_s[:, 32 * tile_i:32 * (tile_i + 1)].unsqueeze(1).broadcast_to([128, H, 32])
        sv = st_s[:, 32 * tile_i:32 * (tile_i + 1)].unsqueeze(1).broadcast_to([128, H, 32])
        r3 = lambda ap: ap.rearrange("t (h two f) -> t h two f", h=H, two=2)
        xe, xo = r3(src[:])[:, :, 0, :], r3(src[:])[:, :, 1, :]
        re, ro = r3(dst[:])[:, :, 0, :], r3(dst[:])[:, :, 1, :]
        t1 = tmp[:, 0:512].rearrange("t (h f) -> t h f", h=H)
        t2 = tmp[:, 512:1024].rearrange("t (h f) -> t h f", h=H)
        e.tensor_tensor(t1, xe, cv, op=AluOpType.mult)
        e.tensor_tensor(t2, xo, sv, op=AluOpType.mult)
        e.tensor_tensor(re, t1, t2, op=AluOpType.subtract)
        e.tensor_tensor(t1, xe, sv, op=AluOpType.mult)
        e.tensor_tensor(t2, xo, cv, op=AluOpType.mult)
        return e.tensor_tensor(ro, t1, t2, op=AluOpType.add)

    with contextlib.ExitStack() as p0:
        wf = [p0.enter_context(nc.sbuf_tensor(f"wf_{i}", [128, DH], F32)) for i in range(2)]
        wqf = [p0.enter_context(nc.sbuf_tensor(f"wqf_{i}", [128, DH], F32)) for i in range(2)]
        wqb = [p0.enter_context(nc.sbuf_tensor(f"wqb_{i}", [128, D], BF16)) for i in range(2)]
        ps_tot = p0.enter_context(nc.psum_tensor("ps_tot", [2, 512], F32))

        dma("sync", ct_s[:], ct_in, "s_ld")
        dma("sync", st_s[:], st_in, "s_ld", waits=[("s_ld", p.now("s_ld"))])
        dma("sync", eye_s[:], eye_in, "s_ld", waits=[("s_ld", p.now("s_ld"))])
        c_consts = dma("sync", ones_s[:], ones_in, "s_ld",
                       waits=[("s_ld", p.now("s_ld"))])
        c_onesb = p.on("vector", lambda e: e.tensor_copy(onesb_s[:], ones_s[:, 0:1]),
                       waits=[("s_ld", c_consts)], inc=("s_v", 1), fence=True)
        for kk in range(KCH):
            dma("gpsimd", wq_s[:, kk * D:(kk + 1) * D],
                wqkvT[128 * kk:128 * (kk + 1), 0:D], "s_wc",
                waits=[("s_wc", p.now("s_wc"))])
            dma("gpsimd", wout_s[:, kk * D:(kk + 1) * D],
                woutT[128 * kk:128 * (kk + 1), :], "s_wc",
                waits=[("s_wc", p.now("s_wc"))])
        c_wcast = p.now("s_wc")

        p.on("vector", lambda e: e.memset(cols[:, 0:2], 0.0), waits=[("s_ld", c_consts)])
        c_wdma = {}
        for idx, (nch, src, width) in enumerate(((KCH, w1T, DH), (32, w2T, D))):
            for r in range(nch):
                key = (idx, r)
                w = [("s_v", c_wdma.get((idx, r - 2)))] if r >= 2 or idx > 0 else ()
                if idx == 1 and r < 2:
                    w = [("s_v", c_wdma.get((0, KCH - 2 + r)))]
                c = dma("sync", wf[r % 2][:, 0:width], src[128 * r:128 * (r + 1), :],
                        "s_l0" if r % 2 == 0 else "s_l1", waits=w)

                lsem = "s_l0" if r % 2 == 0 else "s_l1"
                if r % 2 == 1:
                    c_ab = p.on("scalar", lambda e, r=r, width=width: e.activation(
                        wqf[1][:, 0:width], wf[r % 2][:, 0:width], AF.Abs,
                        accum_out=cols[:, 3:4]),
                        waits=[(lsem, c), ("s_v", c_wdma.get((idx, r - 2)) or
                               (c_wdma.get((0, KCH - 1)) if idx == 1 else None) or 0)],
                        inc=("s_s", 1), fence=True)
                    c_wdma[key] = p.on("vector", lambda e, idx=idx: e.tensor_tensor(
                        cols[:, idx:idx + 1], cols[:, idx:idx + 1], cols[:, 3:4],
                        op=AluOpType.add),
                        waits=[("s_s", c_ab)], inc=("s_v", 1), fence=True)
                else:
                    def absacc(e, r=r, idx=idx, width=width):
                        e.tensor_reduce(cols[:, 2:3], wf[r % 2][:, 0:width], axis=AX.X,
                                        op=AluOpType.add, apply_absolute_value=True)
                        e.drain()
                        return e.tensor_tensor(cols[:, idx:idx + 1], cols[:, idx:idx + 1],
                                               cols[:, 2:3], op=AluOpType.add)
                    c_wdma[key] = p.on("vector", absacc,
                                       waits=[(lsem, c)],
                                       inc=("s_v", 1), fence=True)
        c_sums = p.now("s_v")
        c_tot = p.on("tensor", lambda e: e.matmul(
            ps_tot[0:1, 0:2], ones_s[:, 0:1], cols[:, 0:2], start=True, stop=True),
            waits=[("s_v", c_sums), ("s_ld", c_consts)], inc=("s_t", 1))

        def mk_scal(e):
            e.tensor_scalar(scalc[0:1, 4:6], ps_tot[0:1, 0:2], 1.0 / (D * DH), EPS,
                            op0=AluOpType.mult, op1=AluOpType.max)
            e.drain()
            e.reciprocal(scalc[0:1, 0:1], scalc[0:1, 4:5])
            e.reciprocal(scalc[0:1, 2:3], scalc[0:1, 5:6])
            e.tensor_scalar(scalc[0:1, 1:2], scalc[0:1, 4:5], 1.0 / 127.0, None,
                            op0=AluOpType.mult)
            return e.tensor_scalar(scalc[0:1, 3:4], scalc[0:1, 5:6], 1.0 / 127.0, None,
                                   op0=AluOpType.mult)
        c_scal = p.on("vector", mk_scal, waits=[("s_t", c_tot)], inc=("s_v", 1),
                      fence=True)
        c = dma("gpsimd", scal_hbm.ap()[0:1, 0:6], scalc[0:1, 0:6], "s_sc",
                waits=[("s_v", c_scal)])
        c_scalb = dma("gpsimd", scalc[:, 0:6], scal_hbm.ap()[0:1, 0:6].broadcast_to([128, 6]),
                      "s_sc", waits=[("s_sc", c)])

        c_sq = {"s": {}, "v": {}, "g": {}}
        for idx, (nch, src, width) in enumerate(((KCH, w1T, DH), (32, w2T, D))):
            for r in range(nch):
                gr = r + (KCH if idx == 1 else 0)
                b = gr % 2
                w = ([("s_s", c_sq["s"].get(gr - 2, 0))] if gr >= 2
                     else [("s_v", c_sums)])
                lsem = "s_l0" if b == 0 else "s_l1"
                c = dma("sync", wf[b][:, 0:width], src[128 * r:128 * (r + 1), :],
                        lsem, waits=w)
                sc = scalc[:, 0:1] if idx == 0 else scalc[:, 2:3]
                c_sq["s"][gr] = p.on(
                    "scalar", lambda e, b=b, width=width, sc=sc: e.activation(
                        wqf[b][:, 0:width], wf[b][:, 0:width], AF.Copy, bias=MAGIC,
                        scale=sc),
                    waits=[(lsem, c), ("s_sc", c_scalb),
                           ("s_v", c_sq["v"].get(gr - 2, 0))],
                    inc=("s_s", 1))

                def clip1(e, b=b, width=width):
                    return e.tensor_scalar(wqf[b][:, 0:width], wqf[b][:, 0:width],
                                           MAGIC, 1.0,
                                           op0=AluOpType.subtract, op1=AluOpType.min)
                wsem = "s_w2w" if b == 0 else "s_w2x"
                p.on("vector", clip1, waits=[("s_s", c_sq["s"][gr]),
                                             (wsem, c_sq["g"].get(gr - 2, 0))])
                if idx == 0:
                    c_sq["v"][gr] = p.on("vector", lambda e, r=r, b=b: e.tensor_scalar(
                        w1q_s[:, DH * r:DH * (r + 1)], wqf[b][:], -1.0, None,
                        op0=AluOpType.max), inc=("s_v", 1))
                else:
                    c_sq["v"][gr] = p.on("vector", lambda e, b=b: e.tensor_scalar(
                        wqb[b][:], wqf[b][:, 0:D], -1.0, None, op0=AluOpType.max),
                        inc=("s_v", 1))
                    c_sq["g"][gr] = dma("gpsimd",
                                        w2q_hbm.ap()[128 * r:128 * (r + 1), :],
                                        wqb[b][:], wsem,
                                        waits=[("s_v", c_sq["v"][gr])])
        if debug:
            dma("gpsimd", dbg["d_scalc"], scalc[:, 0:6], "s_gd", waits=[("s_sc", c_scalb)])
            dma("gpsimd", dbg["d_w1q"], w1q_s[:, 0:DH], "s_gd",
                waits=[("s_v", p.now("s_v"))])
            dma("gpsimd", dbg["d_w2q"], w2q_hbm.ap()[0:128, :], "s_gd",
                waits=[("s_w2w", p.now("s_w2w"))])
        c_p0_v = p.now("s_v")
        c_p0_s = p.now("s_s")
        c_p0_w2w = p.now("s_w2w")
        c_p0_w2x = p.now("s_w2x")

    with contextlib.ExitStack() as p1:
        wkv_s = p1.enter_context(nc.sbuf_tensor("wkv_s", [128, KCH * 2 * D], BF16))
        kf16 = [p1.enter_context(nc.sbuf_tensor(f"kf16_{q}", [128, D], BF16)) for q in range(2)]
        vb16 = [p1.enter_context(nc.sbuf_tensor(f"vb16_{q}", [128, D], BF16)) for q in range(2)]
        kvpart = p1.enter_context(nc.sbuf_tensor("kvpart", [128, D], F32))
        scrD = p1.enter_context(nc.sbuf_tensor("scrD1", [128, D], F32))
        ksrow = p1.enter_context(nc.sbuf_tensor("ksrow", [128, 512], F32))
        ps_kvacc = p1.enter_context(nc.psum_tensor("ps_kvacc", [128, D], F32))
        ps_k = p1.enter_context(nc.psum_tensor("ps_k", [128, D], F32))
        ps_v = p1.enter_context(nc.psum_tensor("ps_v", [128, D], F32))
        ps_xT = p1.enter_context(nc.psum_tensor("ps_xT", [128, D], BF16))
        ps_ks = p1.enter_context(nc.psum_tensor("ps_ks", [128, 512], F32))
        p.on("vector", lambda e: e.memset(kvpart[:], 0.0),
             waits=[("s_s", c_p0_s), ("s_w2w", c_p0_w2w), ("s_w2x", c_p0_w2x)])
        p.on("vector", lambda e: e.memset(ksrow[:], 0.0))

        for kk in range(KCH):
            c_wkv = dma("gpsimd", wkv_s[:, 2 * D * kk:2 * D * (kk + 1)],
                        wqkvT[128 * kk:128 * (kk + 1), D:3 * D], "s_wkv",
                        waits=([("s_w2w", c_p0_w2w), ("s_w2x", c_p0_w2x),
                                ("s_s", c_p0_s),
                                ("s_v", c_p0_v)] if kk == 0
                               else [("s_wkv", p.now("s_wkv"))]))

        p1B = p1.enter_context(nc.sbuf_tensor("p1B", [128, D], F32))
        p1E = p1.enter_context(nc.sbuf_tensor("p1E", [128, D], F32))
        p1C = p1.enter_context(nc.sbuf_tensor("p1C", [128, D], F32))
        p1D = p1.enter_context(nc.sbuf_tensor("p1D", [128, D], F32))
        rot = [p1E, p1B]
        me = [scrC, p1C]
        rl = [scrD, p1D]

        def p1_load(i):
            q = i % 2
            w = ([("s_t", cp.get(("p1_xT", i - 2)))] if i >= 2
                 else [("s_v", c_p0_v), ("s_s", c_p0_s)])
            xsem = "s_g0" if q == 0 else "s_g1"
            cp[("p1_xld", i)] = dma("gpsimd", xb16[q][:],
                                    x_in[128 * i:128 * (i + 1), :], xsem, waits=w)
            cp[("p1_xsem", i)] = xsem

        def p1_frontT(i):
            q = i % 2
            w = [(cp[("p1_xsem", i)], cp[("p1_xld", i)]), ("s_ld", c_consts),
                 ("s_s", cp.get(("p1_xfm_ev", i - 1)))]
            for kk in range(KCH):
                p.on("tensor", lambda e, kk=kk, q=q: e.transpose(
                    ps_xT[:, 128 * kk:128 * (kk + 1)], xb16[q][:, 128 * kk:128 * (kk + 1)],
                    eye_s[:]), waits=w if kk == 0 else ())
            cp[("p1_xT", i)] = p.bump("tensor", "s_t", 1)
            w = [("s_t", cp[("p1_xT", i)]), ("s_t", cp.get(("p1_vmm", i - 2)))]
            cp[("p1_xfm_ev", i)] = p.on(
                "scalar", lambda e, q=q: e.activation(xfm[q][:], ps_xT[:], AF.Copy),
                waits=w, inc=("s_s", 1))

        def p1_kmm(i):
            q = i % 2
            w = [("s_s", cp[("p1_xfm_ev", i)]), ("s_wkv", c_wkv),
                 ("s_v", cp.get(("p1_rope", i - 1)))]
            for ns in range(2):
                for kk in range(KCH):
                    p.on("tensor", lambda e, ns=ns, kk=kk, q=q: e.matmul(
                        ps_k[:, 512 * ns:512 * (ns + 1)], xfm[q][:, 128 * kk:128 * (kk + 1)],
                        wkv_s[:, 2 * D * kk + 512 * ns: 2 * D * kk + 512 * (ns + 1)],
                        start=(kk == 0), stop=(kk == KCH - 1), skip_group_check=True),
                       waits=w if (ns == 0 and kk == 0) else ())
            cp[("p1_kmm", i)] = p.bump("tensor", "s_t", 1)

        def p1_vmm(i):
            q = i % 2
            w = [("s_s", cp.get(("p1_vev", i - 1)))]
            for ns in range(2):
                for kk in range(KCH):
                    p.on("tensor", lambda e, ns=ns, kk=kk, q=q: e.matmul(
                        ps_v[:, 512 * ns:512 * (ns + 1)], xfm[q][:, 128 * kk:128 * (kk + 1)],
                        wkv_s[:, 2 * D * kk + D + 512 * ns: 2 * D * kk + D + 512 * (ns + 1)],
                        start=(kk == 0), stop=(kk == KCH - 1), skip_group_check=True),
                       waits=w if (ns == 0 and kk == 0) else ())
            cp[("p1_vmm", i)] = p.bump("tensor", "s_t", 1)

        def p1_rope(i):
            q = i % 2
            cp[("p1_rope", i)] = p.on(
                "vector", lambda e, i=i, q=q: rope_ops(e, ps_k, rot[q], i, tmp=scrA),
                waits=[("s_t", cp[("p1_kmm", i)])], inc=("s_v", 1))
            cp[("p1_min", i)] = p.on("vector", lambda e, q=q: e.tensor_scalar(
                me[q][:], rot[q][:], 0.0, None, op0=AluOpType.min), inc=("s_v", 1))

        def p1_feat(i):
            q = i % 2
            c_exp = p.on("scalar", lambda e, q=q: e.activation(me[q][:], me[q][:], AF.Exp),
                         waits=[("s_v", cp[("p1_min", i)])], inc=("s_s", 1))
            c_rel = p.on("scalar", lambda e, q=q: e.activation(rl[q][:], rot[q][:], AF.Relu),
                         inc=("s_s", 1))
            cp[("p1_kf", i)] = p.on("vector", lambda e, q=q: e.tensor_tensor(
                kf16[q][:], me[q][:], rl[q][:], op=AluOpType.add),
                waits=[("s_s", c_rel), ("s_t", cp.get(("p1_kv", i - 2)))], inc=("s_v", 1))
            cp[("p1_vev", i)] = p.on("scalar", lambda e, q=q: e.activation(
                vb16[q][:], ps_v[:], AF.Copy),
                waits=[("s_t", cp[("p1_vmm", i)]), ("s_t", cp.get(("p1_kv", i - 2)))],
                inc=("s_s", 1))

        def p1_acc(i):
            q = i % 2
            w = [("s_v", cp[("p1_kf", i)]), ("s_s", cp[("p1_vev", i)]),
                 ("s_v", c_onesb), ("s_v", cp.get(("p1_kvadd", i - 1)))]
            for c8 in range(KCH):
                p.on("tensor", lambda e, c8=c8, q=q: e.matmul(
                    ps_kvacc[:, 128 * c8:128 * (c8 + 1)], kf16[q][:, 128 * c8:128 * (c8 + 1)],
                    vb16[q][:, 128 * c8:128 * (c8 + 1)],
                    start=True, stop=True, skip_group_check=True),
                   waits=w if c8 == 0 else ())
            for ns, base in ((0, 0), (1, 32)):
                p.on("tensor", lambda e, ns=ns, base=base, q=q: e.matmul(
                    ps_ks[base:base + 1, 0:512], onesb_s[:],
                    kf16[q][:, 512 * ns:512 * (ns + 1)],
                    start=True, stop=True, skip_group_check=True,
                    tile_position=(0, base)))
            cp[("p1_kv", i)] = p.bump("tensor", "s_t", 1)

            def kvadd(e):
                e.tensor_tensor(kvpart[:], kvpart[:], ps_kvacc[:], op=AluOpType.add)
                e.tensor_tensor(ksrow[0:1, :], ksrow[0:1, :], ps_ks[0:1, :],
                                op=AluOpType.add)
                e.tensor_tensor(ksrow[32:33, :], ksrow[32:33, :], ps_ks[32:33, :],
                                op=AluOpType.add)
                return e.drain()
            cp[("p1_kvadd", i)] = p.on("vector", kvadd,
                                       waits=[("s_t", cp[("p1_kv", i)])],
                                       inc=("s_v", 1))

        p1_load(0)
        p1_frontT(0)
        p1_kmm(0)
        p1_vmm(0)
        for i in range(NT):
            p1_rope(i)
            if i >= 1:
                p1_acc(i - 1)
            if i + 1 < NT:
                p1_load(i + 1)
                p1_frontT(i + 1)
                p1_kmm(i + 1)
            p1_feat(i)
            if i + 1 < NT:
                p1_vmm(i + 1)
        p1_acc(NT - 1)
        c_p1_t = p.now("s_t")
        ck1 = cp[("p1_kvadd", NT - 1)]
        dma("gpsimd", cc_in.ap()[0:128, :], kvpart[:], "s_ci", waits=[("s_v", ck1)])
        dma("gpsimd", cc_in.ap()[128:129, 0:512], ksrow[0:1, :], "s_ci",
            waits=[("s_ci", p.now("s_ci"))])
        c_ccin = dma("gpsimd", cc_in.ap()[128:129, 512:1024], ksrow[32:33, :], "s_ci",
                     waits=[("s_ci", p.now("s_ci"))])
        c_cc = p.on("gpsimd", lambda e: e.collective_compute(
            "AllReduce", mybir.AluOpType.add,
            replica_groups=[[0, 1, 2, 3], [4, 5, 6, 7]],
            ins=[cc_in.ap().opt()], outs=[cc_out.ap().opt()]),
            waits=[("s_ci", c_ccin)], inc=("s_cc", 1))
        c_kvrd = dma("gpsimd", kvpart[:], cc_out.ap()[0:128, :], "s_kvr",
                     waits=[("s_cc", c_cc)])
        c_ksb = dma("gpsimd", ksum_b[:], cc_out.ap()[128:129, :].broadcast_to([128, D]),
                    "s_ksb")
        if debug:
            dma("gpsimd", dbg["d_ccin"], cc_in.ap(), "s_gd", waits=[("s_ksb", c_ksb)])
            dma("gpsimd", dbg["d_ccout"], cc_out.ap(), "s_gd")
            dma("gpsimd", dbg["d_kv"], kvpart[:], "s_gd")
            dma("gpsimd", dbg["d_ks"], ksum_b[:], "s_gd")
        p.on("vector", lambda e: e.memset(kvaug[:], 0.0), waits=[("s_kvr", c_kvrd)])
        for c8 in range(KCH):
            def cpk(e, c8=c8):
                e.tensor_copy(kvaug[0:64, 128 * c8:128 * c8 + 64],
                              kvpart[0:64, 128 * c8:128 * c8 + 64])
                return e.tensor_copy(kvaug[64:128, 128 * c8 + 64:128 * (c8 + 1)],
                                     kvpart[64:128, 128 * c8 + 64:128 * (c8 + 1)])
            p.on("vector", cpk)
        c_kvaug = p.bump("vector", "s_v", 1, fence=True)

    with contextlib.ExitStack() as p2:
        xb32 = [p2.enter_context(nc.sbuf_tensor(f"xb32_{q}", [128, D], F32)) for q in range(2)]
        x1b = [p2.enter_context(nc.sbuf_tensor(f"x1b_{q}", [128, D], F32)) for q in range(2)]
        h_act = p2.enter_context(nc.sbuf_tensor("h_act", [128, DH], F32))
        hqfm = [p2.enter_context(nc.sbuf_tensor(f"hqfm_{q}", [128, DH], BF16)) for q in range(2)]
        w2b = [p2.enter_context(nc.sbuf_tensor(f"w2b_{b}", [128, D], BF16)) for b in range(4)]
        yA = p2.enter_context(nc.sbuf_tensor("yA", [128, D], F32))
        u2 = [scrA, p2.enter_context(nc.sbuf_tensor("u2_1", [128, D], F32))]
        yB = p2.enter_context(nc.sbuf_tensor("yB", [128, D], F32))
        bs1 = p2.enter_context(nc.sbuf_tensor("bs1", [128, D], BF16))
        bs2 = p2.enter_context(nc.sbuf_tensor("bs2", [128, D], BF16))
        bs3 = p2.enter_context(nc.sbuf_tensor("bs3", [128, D], BF16))
        bs4 = p2.enter_context(nc.sbuf_tensor("bs4", [128, D], BF16))
        bs5 = p2.enter_context(nc.sbuf_tensor("bs5", [128, D], BF16))
        bs6 = p2.enter_context(nc.sbuf_tensor("bs6", [128, D], BF16))
        ps_T = p2.enter_context(nc.psum_tensor("ps_T", [128, 2048], BF16))
        ps_q = p2.enter_context(nc.psum_tensor("ps_q", [128, D], F32))
        ps_at = p2.enter_context(nc.psum_tensor("ps_at", [128, D], F32))
        ps_f = p2.enter_context(nc.psum_tensor("ps_f", [128, D], F32))
        DUMP = h_act[:, 2048:3072]

        def tA_u(i):
            q = i % 2
            w = [("s_t", cp[("p2_womm", i)]), ("s_s", cp.get(("p2_s2", i - 1)))]
            c_u = p.on("vector", lambda e, q=q: e.tensor_tensor(
                scrC[:], ps_q[:], xb32[q][:], op=AluOpType.add),
                waits=w, inc=("s_v", 1))
            if i + 2 < n_tiles:
                xsemn = "s_xa" if i % 2 == 0 else "s_xb"
                cp[("p2_xld", i + 2)] = dma(
                    "scalar", xb32[i % 2][:], x_in[128 * (i + 2):128 * (i + 3), :],
                    xsemn, waits=[("s_v", c_u), ("s_ksb", c_ksb)])
            cp[("p2_u", i)] = c_u
            cp[("p2_psq_free", i)] = c_u
            if debug and i == DBG_TILE:
                cdbg = dma("gpsimd", dbg["d_u"], scrC[:], "s_gd", waits=[("s_v", c_u)])
                cdbg2 = dma("gpsimd", dbg["d_att"], bs5[:], "s_gd")
                c_u = p.on("vector", lambda e: e.memset(cols[:, 30:31], 0.0),
                           waits=[("s_gd", cdbg2)], inc=("s_v", 1))
                cp[("p2_u", i)] = c_u
                cp[("p2_psq_free", i)] = c_u

        def tA_front(i):
            q = i % 2
            xsem = "s_xa" if q == 0 else "s_xb"
            c_xld = cp.pop(("p2_xld", i), None)
            if c_xld is None:
                w = [("s_v", cp.get(("p2_u", i - 2))), ("s_ksb", c_ksb)]
                c_xld = dma("scalar", xb32[q][:], x_in[128 * i:128 * (i + 1), :],
                            xsem, waits=w)
            w = [(xsem, c_xld), ("s_t", cp.get(("p2_xT", i - 2)))]
            c_xb = p.on("vector", lambda e, q=q: e.tensor_copy(xb16[q][:], xb32[q][:]),
                        waits=w, inc=("s_v", 1))
            w = [("s_v", c_xb), ("s_s", cp.get(("p2_xfm_ev", i - 1))),
                 ("s_v", c_kvaug)]
            for kk in range(KCH):
                p.on("tensor", lambda e, kk=kk, q=q: e.transpose(
                    ps_T[:, 128 * kk:128 * (kk + 1)], xb16[q][:, 128 * kk:128 * (kk + 1)],
                    eye_s[:]), waits=w if kk == 0 else ())
            cp[("p2_xT", i)] = p.bump("tensor", "s_t", 1)
            w = [("s_t", cp[("p2_xT", i)]), ("s_t", cp.get(("p2_qmm", i - 2)))]
            cp[("p2_xfm_ev", i)] = p.on(
                "scalar", lambda e, q=q: e.activation(xfm[q][:], ps_T[:, 0:1024], AF.Copy),
                waits=w, inc=("s_s", 1))
            w = [("s_s", cp[("p2_xfm_ev", i)]), ("s_wc", c_wcast),
                 ("s_v", cp.get(("p2_psq_free", i - 1)))]
            for ns in range(2):
                for kk in range(KCH):
                    p.on("tensor", lambda e, ns=ns, kk=kk, q=q: e.matmul(
                        ps_q[:, 512 * ns:512 * (ns + 1)], xfm[q][:, 128 * kk:128 * (kk + 1)],
                        wq_s[:, D * kk + 512 * ns: D * kk + 512 * (ns + 1)],
                        start=(kk == 0), stop=(kk == KCH - 1), skip_group_check=True),
                       waits=w if (ns == 0 and kk == 0) else ())
            cp[("p2_qmm", i)] = p.bump("tensor", "s_t", 1)

        def tA_mid1(i):
            w = [("s_t", cp[("p2_qmm", i)])]
            c_rope = p.on("vector", lambda e, i=i: rope_ops(e, ps_q, scrB, i, tmp=yA),
                          waits=w + [("s_s", cp.get(("p2_s2", i - 1)))],
                          inc=("s_v", 1))
            c_min = p.on("vector", lambda e: e.tensor_scalar(
                yB[:], scrB[:], 0.125, 0.0, op0=AluOpType.mult, op1=AluOpType.min),
                inc=("s_v", 1))
            c_exp = p.on("scalar", lambda e: e.activation(yB[:], yB[:], AF.Exp),
                         waits=[("s_v", c_min)], inc=("s_s", 1))
            c_rel = p.on("scalar", lambda e: e.activation(
                yA[:], scrB[:], AF.Relu, scale=0.125), inc=("s_s", 1))
            c_qf = p.on("vector", lambda e: e.tensor_tensor(
                scrB[:], yB[:], yA[:], op=AluOpType.add),
                waits=[("s_s", c_rel)], inc=("s_v", 1))
            cp[("p2_qf", i)] = c_qf
            if debug and i == DBG_TILE:
                cdbg = dma("gpsimd", dbg["d_qf"], scrB[:], "s_gd", waits=[("s_v", c_qf)])
                c_qf = p.on("vector", lambda e: e.memset(cols[:, 30:31], 0.0),
                            waits=[("s_gd", cdbg)], inc=("s_v", 1))
                cp[("p2_qf", i)] = c_qf

            def denom(e):
                e.tensor_tensor(yA[:], scrB[:], ksum_b[:], op=AluOpType.mult)
                e.tensor_reduce(cols[:, 0:16],
                                yA[:].rearrange("t (h f) -> t h f", h=H), axis=AX.X,
                                op=AluOpType.add)
                e.drain()
                e.tensor_scalar(cols[:, 0:16], cols[:, 0:16], 1e-6, None,
                                op0=AluOpType.max)
                e.drain()
                e.reciprocal(cols[:, 16:32], cols[:, 0:16])
                return e.drain()
            p.on("vector", denom, waits=[("s_ksb", c_ksb)])
            w = [("s_t", cp.get(("p2_qfsT", i - 1)))]
            for h in range(H):
                p.on("vector", lambda e, h=h: e.tensor_scalar(
                    bs1[:, 64 * h:64 * (h + 1)], scrB[:, 64 * h:64 * (h + 1)],
                    cols[:, 16 + h:17 + h], None, op0=AluOpType.mult),
                   waits=w if h == 0 else ())
            cp[("p2_qfs", i)] = p.bump("vector", "s_v", 1, fence=True)

        def tA_midT(i):
            w = [("s_v", cp[("p2_qfs", i)]), ("s_s", cp.get("psTqB_ev"))]
            for kk in range(KCH):
                p.on("tensor", lambda e, kk=kk: e.transpose(
                    ps_T[:, 1024 + 128 * kk:1024 + 128 * (kk + 1)],
                    bs1[:, 128 * kk:128 * (kk + 1)], eye_s[:]),
                   waits=w if kk == 0 else ())
            cp[("p2_qfsT", i)] = p.bump("tensor", "s_t", 1)
            w = [("s_t", cp[("p2_qfsT", i)]), ("s_t", cp.get(("p2_atmm", i - 1)))]
            c_qff = p.on("scalar", lambda e: e.activation(bs4[:], ps_T[:, 1024:2048], AF.Copy),
                         waits=w, inc=("s_s", 1))
            cp["psTqB_ev"] = c_qff
            w = [("s_s", c_qff), ("s_v", c_kvaug),
                 ("s_v", cp.get("psat_free")), ("s_s", cp.get(("p2_ate", i - 1)))]
            for c8 in range(KCH):
                p.on("tensor", lambda e, c8=c8: e.matmul(
                    ps_at[:, 128 * c8:128 * (c8 + 1)], kvaug[:, 128 * c8:128 * (c8 + 1)],
                    bs4[:, 128 * c8:128 * (c8 + 1)], start=True, stop=True,
                    skip_group_check=True), waits=w if c8 == 0 else ())
            cp[("p2_atmm", i)] = p.bump("tensor", "s_t", 1)
            w = [("s_t", cp[("p2_atmm", i)]), ("s_t", cp.get(("p2_womm", i - 1)))]
            c_ate = p.on("scalar", lambda e: e.activation(bs5[:], ps_at[:], AF.Copy),
                         waits=w, inc=("s_s", 1))
            cp[("p2_ate", i)] = c_ate
            w = [("s_s", c_ate), ("s_v", cp[("p2_qf", i)])]
            for ns in range(2):
                for kk in range(KCH):
                    p.on("tensor", lambda e, ns=ns, kk=kk: e.matmul(
                        ps_q[:, 512 * ns:512 * (ns + 1)], bs5[:, 128 * kk:128 * (kk + 1)],
                        wout_s[:, D * kk + 512 * ns: D * kk + 512 * (ns + 1)],
                        start=(kk == 0), stop=(kk == KCH - 1), skip_group_check=True),
                       waits=w if (ns == 0 and kk == 0) else ())
            cp[("p2_womm", i)] = p.bump("tensor", "s_t", 1)

        def tA_backA(i):
            q = i % 2
            s = i % 2
            c_u = cp[("p2_u", i)]
            p.on("scalar", lambda e: e.activation(
                DUMP, scrC[:], AF.Identity, accum_out=stats[:, 0:1]),
                waits=[("s_v", c_u)], inc=("s_s", 1))
            c_s2 = p.on("scalar", lambda e: e.activation(
                DUMP, scrC[:], AF.Square, accum_out=stats[:, 1:2]),
                inc=("s_s", 1), fence=True)
            cp[("p2_s2", i)] = c_s2

            def ln1cols(e):
                e.tensor_scalar(stats[:, 2:3], stats[:, 0:1], 1.0 / D, None,
                                op0=AluOpType.mult)
                e.tensor_tensor(stats[:, 3:4], stats[:, 2:3], stats[:, 2:3],
                                op=AluOpType.mult)
                e.tensor_scalar(stats[:, 4:5], stats[:, 1:2], 1.0 / D, None,
                                op0=AluOpType.mult)
                e.tensor_tensor(stats[:, 4:5], stats[:, 4:5], stats[:, 3:4],
                                op=AluOpType.subtract)
                e.tensor_scalar(stats[:, 3:4], stats[:, 4:5], EPS, None,
                                op0=AluOpType.add)
                e.tensor_tensor(stats[:, 5:6], stats[:, 3:4], ones_s[:, 1:2],
                                op=AluOpType.pow)
                return e.tensor_tensor(stats[:, 18:19], stats[:, 3:4], ones_s[:, 1:2],
                                       op=AluOpType.pow)
            c_rs = p.on("gpsimd", ln1cols, waits=[("s_s", c_s2)], inc=("s_gc", 1))
            c_x1 = p.on("vector", lambda e, s=s: e.tensor_scalar(
                x1b[s][:], scrC[:], stats[:, 2:3], stats[:, 5:6],
                op0=AluOpType.subtract, op1=AluOpType.mult),
                waits=[("s_gc", c_rs)], inc=("s_v", 1))

            c_am1 = p.on("vector", lambda e, s=s: e.tensor_reduce(
                cols[:, 32:33], x1b[s][:], axis=AX.X, op=AluOpType.max,
                apply_absolute_value=True),
                waits=[("s_v", c_x1)], inc=("s_v", 1), fence=True)

            def q1cols(e, s=s):
                e.tensor_scalar(cols[:, 32:33], cols[:, 32:33], EPS, None,
                                op0=AluOpType.max)
                e.tensor_tensor(cols[:, 33:34], cols[:, 32:33], ones_s[:, 3:4],
                                op=AluOpType.pow)
                e.tensor_scalar(cols[:, 33:34], cols[:, 33:34], 127.0, None,
                                op0=AluOpType.mult)
                e.tensor_tensor(cols[:, 34 + s:35 + s], cols[:, 32:33], scalc[:, 1:2],
                                op=AluOpType.mult)
                return e.tensor_tensor(cols[:, 30:31], cols[:, 32:33], scalc[:, 1:2],
                                       op=AluOpType.mult)
            c_q1g = p.on("gpsimd", q1cols, waits=[("s_v", c_am1)], inc=("s_gc", 1))

            def q1(e, s=s):
                e.tensor_scalar(yA[:], x1b[s][:], cols[:, 33:34], MAGIC,
                                op0=AluOpType.mult, op1=AluOpType.add)
                e.tensor_scalar(yA[:], yA[:], MAGIC, 127.0,
                                op0=AluOpType.subtract, op1=AluOpType.min)
                return e.tensor_scalar(bs2[:], yA[:], -128.0, None, op0=AluOpType.max)
            if debug and i == DBG_TILE:
                dma("gpsimd", dbg["d_x1"], x1b[s][:], "s_gd", waits=[("s_v", c_x1)])
            c_q1 = p.on("vector", q1,
                        waits=[("s_gc", c_q1g), ("s_t", cp.get(("p2_xq1T", i - 1)))],
                        inc=("s_v", 1))
            cp[("p2_q1", i)] = c_q1
            w = [("s_v", c_q1), ("s_s", cp.get("psTqB_ev"))]
            for kk in range(KCH):
                p.on("tensor", lambda e, kk=kk: e.transpose(
                    ps_T[:, 1024 + 128 * kk:1024 + 128 * (kk + 1)],
                    bs2[:, 128 * kk:128 * (kk + 1)], eye_s[:]),
                   waits=w if kk == 0 else ())
            cp[("p2_xq1T", i)] = p.bump("tensor", "s_t", 1)
            w = [("s_t", cp[("p2_xq1T", i)]), ("s_t", cp.get(("p2_f1mm", i - 1)))]
            c_xqf = p.on("scalar", lambda e: e.activation(bs6[:], ps_T[:, 1024:2048], AF.Copy),
                         waits=w, inc=("s_s", 1))
            cp["psTqB_ev"] = c_xqf

            gcp = []
            for nsl in range(8):
                half = ps_f[:, 512 * (nsl % 2):512 * (nsl % 2 + 1)]
                w = [("s_s", c_xqf), ("s_v", c_p0_v)]
                if nsl >= 2:
                    w.append(("s_s", gcp[nsl - 2]))
                else:
                    w.append(("s_v", cp.get("psf_free")))
                for kk in range(KCH):
                    p.on("tensor", lambda e, half=half, nsl=nsl, kk=kk: e.matmul(
                        half, bs6[:, 128 * kk:128 * (kk + 1)],
                        w1q_s[:, DH * kk + 512 * nsl: DH * kk + 512 * (nsl + 1)],
                        start=(kk == 0), stop=(kk == KCH - 1), skip_group_check=True),
                       waits=w if kk == 0 else ())
                c_mm = p.bump("tensor", "s_t", 1)
                if nsl == 7:
                    cp[("p2_f1mm", i)] = c_mm
                gw = [("s_t", c_mm), ("s_gc", c_rs)]
                cg = p.on("scalar", lambda e, half=half, nsl=nsl, s=s: e.activation(
                    h_act[:, 512 * nsl:512 * (nsl + 1)], half, AF.Gelu,
                    scale=cols[:, 34 + s:35 + s]),
                    waits=gw, inc=("s_s", 1))
                gcp.append(cg)
            cp[("p2_gl", i)] = gcp[-1]


        def tA_backB(i):
            q = i % 2
            s = i % 2
            def q2(e, s=s):
                return e.tensor_reduce(cols[:, 36:37], h_act[:], axis=AX.X,
                                       op=AluOpType.max, apply_absolute_value=True)

            def q2cols(e, s=s):
                e.tensor_scalar(cols[:, 36:37], cols[:, 36:37], EPS, None,
                                op0=AluOpType.max)
                e.tensor_tensor(cols[:, 37:38], cols[:, 36:37], ones_s[:, 3:4],
                                op=AluOpType.pow)
                e.tensor_scalar(cols[:, 37:38], cols[:, 37:38], 127.0, None,
                                op0=AluOpType.mult)
                e.tensor_tensor(cols[:, 38 + s:39 + s], cols[:, 36:37],
                                scalc[:, 3:4], op=AluOpType.mult)
                return e.tensor_tensor(cols[:, 30:31], cols[:, 36:37],
                                       scalc[:, 3:4], op=AluOpType.mult)
            if debug and i == DBG_TILE:
                cdbg = dma("gpsimd", dbg["d_xq1"], bs2[:], "s_gd",
                           waits=[("s_v", cp[("p2_q1", i)])])
                cdbg2 = dma("gpsimd", dbg["d_hact"], h_act[:], "s_gd",
                            waits=[("s_s", cp[("p2_gl", i)])])
            c_q2 = p.on("vector", q2, waits=[("s_s", cp[("p2_gl", i)])], inc=("s_v", 1),
                        fence=True)
            c_q2 = p.on("gpsimd", q2cols, waits=[("s_v", c_q2)], inc=("s_gc", 1))
            if debug and i == DBG_TILE:
                c_q2 = p.on("vector", lambda e: e.memset(cols[:, 30:31], 0.0),
                            waits=[("s_gd", cdbg2)], inc=("s_v", 1))
            prev_T = cp.get(("p2_hqT", i - 1))
            osem_q = "s_oa" if s == 0 else "s_ob"
            w_st = (osem_q, cp.get(("p2_out", i // 2 - 1, s)))
            for nsl in range(4):
                def qs(e, nsl=nsl, s=s):
                    e.tensor_scalar(u2[s][:], h_act[:, 1024 * nsl:1024 * (nsl + 1)],
                                    cols[:, 37:38], MAGIC, op0=AluOpType.mult,
                                    op1=AluOpType.add)
                    e.tensor_scalar(u2[s][:], u2[s][:], MAGIC, 127.0,
                                    op0=AluOpType.subtract, op1=AluOpType.min)
                    return e.tensor_scalar(bs3[:], u2[s][:], -128.0, None,
                                           op0=AluOpType.max)
                w = [("s_gc", c_q2), ("s_t", prev_T), w_st]
                c_qs = p.on("vector", qs, waits=w, inc=("s_v", 1))
                w = [("s_v", c_qs), ("s_s", cp.get("psTqB_ev"))]
                for kk in range(KCH):
                    p.on("tensor", lambda e, kk=kk: e.transpose(
                        ps_T[:, 1024 + 128 * kk:1024 + 128 * (kk + 1)],
                        bs3[:, 128 * kk:128 * (kk + 1)], eye_s[:]),
                       waits=w if kk == 0 else ())
                prev_T = p.bump("tensor", "s_t", 1)
                cp["psTqB_ev"] = p.on(
                    "scalar", lambda e, s=s, nsl=nsl: e.activation(
                        hqfm[s][:, 1024 * nsl:1024 * (nsl + 1)], ps_T[:, 1024:2048],
                        AF.Copy), waits=[("s_t", prev_T)], inc=("s_s", 1))
            cp[("p2_hqT", i)] = prev_T
            cp[("p2_hq_ready", i)] = p.now("s_s")
            if debug and i == DBG_TILE:
                dma("gpsimd", dbg["d_hq"], hqfm[DBG_TILE % 2][:], "s_gd",
                    waits=[("s_s", cp[("p2_hq_ready", i)])])

        def pair_mm(j):
            i0, i1 = 2 * j, 2 * j + 1
            for kk in range(32):
                b = kk % 4
                if kk >= 4:
                    w = [("s_t", cp.get(("p2_w2u", kk - 4, j)))]
                elif j >= 1:
                    w = [("s_t", cp.get(("p2_w2u", 28 + kk, j - 1)))]
                else:
                    w = [("s_ksb", c_ksb)]
                wsem = f"s_w2{b}"
                c_wl = dma("sync", w2b[b][:], w2q_hbm.ap()[128 * kk:128 * (kk + 1), :],
                           wsem, waits=w)
                w = [(wsem, c_wl), ("s_s", cp[("p2_hq_ready", i1)])]
                if kk == 0:
                    w.append(("s_s", cp.get(("p2_ate", i1 + 1))))
                first = True
                for s, tgt in ((0, ps_f), (1, ps_at)):
                    for nh in range(2):
                        p.on("tensor", lambda e, tgt=tgt, s=s, kk=kk, nh=nh: e.matmul(
                            tgt[:, 512 * nh:512 * (nh + 1)],
                            hqfm[s][:, 128 * kk:128 * (kk + 1)],
                            w2b[kk % 4][:, 512 * nh:512 * (nh + 1)],
                            start=(kk == 0), stop=(kk == 31), skip_group_check=True),
                           waits=w if first else ())
                        first = False
                cp[("p2_w2u", kk, j)] = p.bump("tensor", "s_t", 1)
            c_f2 = p.now("s_t")
            cp[("p2_f2mm", i0)] = c_f2
            cp[("p2_f2mm", i1)] = c_f2
            cp[("p2_f2", j)] = c_f2

        def pair_epA(j):
            c_f2 = cp[("p2_f2", j)]
            for s, tgt in ((0, ps_f), (1, ps_at)):
                b2 = 6 + 6 * s
                osem = "s_oa" if s == 0 else "s_ob"
                w = [("s_t", c_f2), (osem, cp.get(("p2_out", j - 1, s)))]

                def evac(e, s=s, tgt=tgt):
                    return e.scalar_tensor_tensor(u2[s][:], tgt[:], cols[:, 38 + s:39 + s],
                                                  x1b[s][:], op0=AluOpType.mult,
                                                  op1=AluOpType.add)
                c_ev = p.on("vector", evac, waits=w, inc=("s_v", 1))
                if s == 0:
                    cp["psf_free"] = c_ev
                else:
                    cp["psat_free"] = c_ev
                if debug and j == DBG_TILE // 2 and s == DBG_TILE % 2:
                    cdbg = dma("gpsimd", dbg["d_u2"], u2[s][:], "s_gd", waits=[("s_v", c_ev)])
                    c_ev = p.on("vector", lambda e: e.memset(cols[:, 30:31], 0.0),
                                waits=[("s_gd", cdbg)], inc=("s_v", 1))
                p.on("scalar", lambda e, b2=b2, s=s: e.activation(
                    DUMP, u2[s][:], AF.Identity, accum_out=stats[:, b2:b2 + 1]),
                    waits=[("s_v", c_ev)], inc=("s_s", 1))
                c_es2 = p.on("scalar", lambda e, b2=b2, s=s: e.activation(
                    DUMP, u2[s][:], AF.Square, accum_out=stats[:, b2 + 1:b2 + 2]),
                    inc=("s_s", 1), fence=True)

                def ln2cols(e, b2=b2):
                    e.tensor_scalar(stats[:, b2 + 2:b2 + 3], stats[:, b2:b2 + 1],
                                    1.0 / D, None, op0=AluOpType.mult)
                    e.tensor_tensor(stats[:, b2 + 4:b2 + 5], stats[:, b2 + 2:b2 + 3],
                                    stats[:, b2 + 2:b2 + 3], op=AluOpType.mult)
                    e.tensor_scalar(stats[:, b2 + 3:b2 + 4], stats[:, b2 + 1:b2 + 2],
                                    1.0 / D, None, op0=AluOpType.mult)
                    e.tensor_tensor(stats[:, b2 + 3:b2 + 4], stats[:, b2 + 3:b2 + 4],
                                    stats[:, b2 + 4:b2 + 5], op=AluOpType.subtract)
                    e.tensor_scalar(stats[:, b2 + 3:b2 + 4],
                                    stats[:, b2 + 3:b2 + 4], EPS, None,
                                    op0=AluOpType.add)
                    e.tensor_tensor(stats[:, b2 + 5:b2 + 6], stats[:, b2 + 3:b2 + 4],
                                    ones_s[:, 1:2], op=AluOpType.pow)
                    return e.tensor_tensor(stats[:, b2 + 4:b2 + 5],
                                           stats[:, b2 + 3:b2 + 4],
                                           ones_s[:, 1:2], op=AluOpType.pow)
                c_rs2 = p.on("gpsimd", ln2cols, waits=[("s_s", c_es2)],
                             inc=("s_gc", 1))
                cp[("p2_rs2", j, s)] = c_rs2

        def pair_epB(j):
            for s in range(2):
                i = 2 * j + s
                b2 = 6 + 6 * s
                osem = "s_oa" if s == 0 else "s_ob"

                def fin(e, b2=b2, s=s):
                    return e.tensor_scalar(u2[s][:], u2[s][:], stats[:, b2 + 2:b2 + 3],
                                           stats[:, b2 + 5:b2 + 6],
                                           op0=AluOpType.subtract, op1=AluOpType.mult)
                c_fin = p.on("vector", fin,
                             waits=[("s_gc", cp[("p2_rs2", j, s)])], inc=("s_v", 1),
                             fence=True)
                cp[("p2_out", j, s)] = dma(
                    "scalar", y_out[128 * i:128 * (i + 1), :], u2[s][:], osem,
                    waits=[("s_v", c_fin)])
                if dual_store:
                    cp[("p2_out", j, s)] = dma(
                        "scalar", y2_out[128 * i:128 * (i + 1), :], u2[s][:], osem,
                        waits=[(osem, p.now(osem))])

        tA_front(0)
        tA_mid1(0)
        tA_midT(0)
        for i in range(NT):
            tA_u(i)
            if i + 1 < NT:
                tA_front(i + 1)
            if i % 2 == 0 and i >= 2:
                pair_epA(i // 2 - 1)
            tA_backA(i)
            if i % 2 == 0 and i >= 2:
                pair_epB(i // 2 - 1)
            if i + 1 < NT:
                tA_mid1(i + 1)
                tA_midT(i + 1)
            tA_backB(i)
            if i % 2 == 1:
                pair_mm(i // 2)
        pair_epA(NT // 2 - 1)
        pair_epB(NT // 2 - 1)

        p.on("scalar", lambda e: None,
             waits=[("s_oa", p.now("s_oa")), ("s_ob", p.now("s_ob"))])

    sem_names = set()
    for eng in p.ENGINES:
        for waits, fn, inc, fence in p.items[eng]:
            for sname, v in waits:
                sem_names.add(sname)
            if inc:
                sem_names.add(inc[0])
    with contextlib.ExitStack() as semstack:
        sems = {sn: semstack.enter_context(nc.semaphore(sn)) for sn in sorted(sem_names)}
        with nc.Block() as block:
            p.emit(block, sems)
    es.close()
    return nc


_ROPE_BASE = 10000.0


def _tables(n_tiles, pos0):
    t = np.arange(pos0, pos0 + n_tiles * 128, dtype=np.float32)
    inv = (1.0 / _ROPE_BASE ** (np.arange(0, HD, 2, dtype=np.float32) / HD)).astype(np.float32)
    fr = (t[:, None] * inv[None, :]).astype(np.float32)
    cos = np.cos(fr.astype(np.float64)).astype(np.float32)
    sin = np.sin(fr.astype(np.float64)).astype(np.float32)

    def tile(a):
        return np.ascontiguousarray(
            a.reshape(n_tiles, 128, 32).transpose(1, 0, 2).reshape(128, n_tiles * 32))
    return tile(cos), tile(sin)


def kernel(L0_attn_x, L0_attn_w_qkv, L0_attn_w_out, L0_ffn_w1, L0_ffn_w2):
    B, T, _ = L0_attn_x.shape
    tokens_per_core = B * T // NCORES
    n_tiles = tokens_per_core // 128
    cores_per_batch = NCORES // B

    x_flat = np.ascontiguousarray(np.asarray(L0_attn_x, dtype=np.float32).reshape(B * T, D))
    wqkvT = np.ascontiguousarray(np.asarray(L0_attn_w_qkv, dtype=np.float32).T)
    woutT = np.ascontiguousarray(np.asarray(L0_attn_w_out, dtype=np.float32).T)
    w1T = np.ascontiguousarray(np.asarray(L0_ffn_w1, dtype=np.float32).T)
    w2T = np.ascontiguousarray(np.asarray(L0_ffn_w2, dtype=np.float32).T)
    eye = np.eye(128, dtype=ml_dtypes.bfloat16)
    onesc = np.ascontiguousarray(
        np.stack([np.ones(128, np.float32), np.full(128, -0.5, np.float32),
                  np.full(128, 127.0, np.float32), np.full(128, -1.0, np.float32)],
                 axis=1))

    nc = build(n_tiles)
    in_maps = []
    for c in range(NCORES):
        pos0 = (c % cores_per_batch) * tokens_per_core
        ct, st = _tables(n_tiles, pos0)
        in_maps.append({
            "x": x_flat[c * tokens_per_core:(c + 1) * tokens_per_core],
            "wqkvT": wqkvT, "woutT": woutT, "w1T": w1T, "w2T": w2T,
            "ctab": ct, "stab": st, "eye": eye, "onesc": onesc,
        })
    res = run_bass_kernel_spmd(nc, in_maps, core_ids=list(range(NCORES)))
    out = np.concatenate([np.asarray(r["y"]) for r in res.results], axis=0)
    return out.reshape(B, T, D).astype(np.float32)
